# version 08
# baseline (speedup 1.0000x reference)
import numpy as np
import ml_dtypes
import concourse.bass as bass
import concourse.mybir as mybir
from concourse.bass_utils import run_bass_kernel_spmd
from concourse.bass_isa import ReduceOp
from concourse.library_config import attn as ATTN_LIB
from concourse.library_overlay import lower_extended_insts

F32 = mybir.dt.float32
BF16 = mybir.dt.bfloat16
I32 = mybir.dt.int32
AF = mybir.ActivationFunctionType
ALU = mybir.AluOpType
AX = mybir.AxisListType

OFFSETS = [(1 << j) * tau for j in range(11) for tau in range(4)]
DIST = sorted(set(OFFSETS))
NDIST = len(DIST)
GROUPS = [[i for i, o in enumerate(OFFSETS) if o == d] for d in DIST]
MAXG = max(len(g) for g in GROUPS)
H, HD, D, DH = 16, 64, 1024, 4096
NSEQ = 16384
U = 2048
HF = 32
HK = 3072 + HF
EXTN = HK + U + HF
TAN = U + 2 * HF
CH = 512
W7 = 7
WINC = W7 * CH

DEBUG_SCRATCH = False

PERM = np.zeros(D, np.int64)
for _g in range(8):
    for _p in range(128):
        PERM[128 * _g + _p] = 64 * (_p // 8) + 8 * _g + (_p % 8)


def _chunks(total, ch):
    out, a = [], 0
    while a < total:
        out.append((a, min(a + ch, total)))
        a = min(a + ch, total)
    return out


CHUNKS_E = _chunks(EXTN, CH)
TA0 = HK - HF
CHUNKS_TA = [(a, b) for (a, b) in CHUNKS_E if a >= TA0]
CHUNKS_P1 = _chunks(TA0, CH) + [(TA0 + 352 * i, TA0 + 352 * (i + 1)) for i in range(6)]
P3_RANGES = []
for _i in range(6):
    fa, fb = 352 * _i, 352 * (_i + 1)
    P3_RANGES.append((fa, fb, max(fa - 1, HF), min(fb - 1, HF + U)))


class KB:
    def __init__(self, nc):
        self.nc = nc
        self.ops = []
        self.sem = {}
        self.led = {}
        for name in ("dS", "dG", "dC", "tT", "tV", "tS", "tG"):
            self.sem[name] = nc.alloc_semaphore(name)
            self.led[name] = 0

    def on(self, eng, fn):
        self.ops.append((eng, fn))

    def dma(self, queue, out, in_):
        sname = {"sync": "dS", "gp": "dG", "sc": "dC"}[queue]
        sem = self.sem[sname]
        self.on(queue, lambda e, o=out, i=in_, s=sem: e.dma_start(out=o, in_=i).then_inc(s, 16))
        self.led[sname] += 16
        return self.led[sname]

    def wait(self, eng, sname, val):
        if val is None or val <= 0:
            return
        sem = self.sem[sname]
        self.on(eng, lambda e, s=sem, v=val: e.wait_ge(s, v))

    def step(self, eng, fn, inc=False):
        if inc:
            sname = {"te": "tT", "ve": "tV", "sc": "tS", "gp": "tG"}[eng]
            sem = self.sem[sname]
            self.on(eng, lambda e, f=fn, s=sem: f(e).then_inc(s, 1))
            self.led[sname] += 1
            return self.led[sname]
        self.on(eng, fn)
        return None

    def barrier_from(self, eng, *semnames):
        for s in semnames:
            self.wait(eng, s, self.led[s])

    def all_sync(self):
        for e in ("te", "ve", "sc", "sync", "gp"):
            for s in ("dS", "dG", "dC", "tT", "tV", "tS", "tG"):
                self.wait(e, s, self.led[s])


def build_program():
    nc = bass.Bass("TRN2", target_bir_lowering=False, debug=False,
                   detect_race_conditions=False)

    din = lambda n, shp, dt: nc.dram_tensor(n, list(shp), dt, kind="ExternalInput").ap()
    xT = din("xT", (D, EXTN), F32)
    wqkvg = din("wqkvg", (D, 4 * D), F32)
    bqkvg = din("bqkvg", (32, 128), F32)
    wout = din("wout", (D, D), F32)
    bout = din("bout", (8, 128), F32)
    w1T = din("w1T", (D, DH), F32)
    w2T = din("w2T", (DH, D), F32)
    cw = din("cw", (96, 128), F32)
    cb = din("cb", (32, 128), F32)
    pb = din("pb", (3, 128, MAXG), F32)
    maskE = din("maskE", (1, EXTN), BF16)
    mask2 = din("mask2", (1, TAN), F32)
    ident_b = din("ident_bf16", (128, 128), BF16)
    ident_f = din("ident_f32", (128, 128), F32)
    ssel_d = din("ssel", (8, 128, 128), BF16)
    asel_d = din("asel", (8, 128, 128), BF16)
    zsel_d = din("zsel", (128, 16), BF16)
    rsel_d = din("rsel", (16, 128), F32)
    onescol_b = din("ones_col_bf16", (128, 1), BF16)
    onescol_f = din("ones_col_f32", (128, 1), F32)
    onesrow_b = din("ones_row_bf16", (1, 128), BF16)
    onesrow_f = din("ones_row_f32", (1, 128), F32)
    n128col = din("n128_col", (128, 1), F32)
    p0125col = din("p0125_col", (128, 1), F32)
    p06col = din("p06_col", (128, 1), F32)

    out_xT = nc.dram_tensor("out_xT", [D, U], F32, kind="ExternalOutput").ap()
    skind = "ExternalOutput" if DEBUG_SCRATCH else "Internal"
    scrt = lambda n, shp, dt: nc.dram_tensor(n, list(shp), dt, kind=skind).ap()
    kD = scrt("kD", (D, EXTN), BF16)
    vD = scrt("vD", (D, EXTN), BF16)
    qD = scrt("qD", (D, TAN), BF16)
    gateD = scrt("gateD", (D, TAN), BF16)
    x1D = scrt("x1D", (D, TAN), F32)
    t2D = scrt("t2D", (DH, D), BF16)
    dbgD = scrt("dbgD", (128, 16), F32)
    dbg2D = scrt("dbg2D", (128, 8 * CH), F32)

    AFN = 53208
    arena = nc.alloc_sbuf_tensor("arena", [128, AFN], F32)

    class Arena:
        def __init__(self):
            self.off = 0
            self.marks = []
            self.peak = 0

        def _adv(self, n):
            o = self.off
            self.off += n
            self.peak = max(self.peak, self.off)
            assert self.off <= AFN, f"arena overflow {self.off}"
            return o

        def f32(self, n):
            return arena[:, (o := self._adv(n)):o + n]

        def bf16(self, n):
            assert n % 2 == 0
            o = self._adv(n // 2)
            return arena[:, o:o + n // 2].bitcast(BF16)

        def i32(self, n):
            o = self._adv(n)
            return arena[:, o:o + n].bitcast(I32)

        def mark(self):
            self.marks.append(self.off)

        def release(self):
            self.off = self.marks.pop()

    A = Arena()
    c_ident_b = A.bf16(128)
    c_ident_f = A.f32(128)
    c_ssel = [A.bf16(128) for _ in range(8)]
    c_asel = [A.bf16(128) for _ in range(8)]
    c_zsel = A.bf16(16)
    c_rsel = A.f32(128)
    c_onescol_b = A.bf16(2)
    c_onescol_f = A.f32(1)
    c_onesrow_b = A.bf16(128)
    c_onesrow_f = A.f32(128)
    c_logB = [A.f32(1) for _ in range(3)]
    c_bcols = A.f32(32)
    c_bqs = A.f32(8)
    c_boutc = A.f32(8)
    c_cwc = A.f32(96)
    c_cbc = A.f32(32)
    c_n128 = A.f32(1)
    c_0125 = A.f32(1)
    c_off = A.f32(1)
    c_offt = A.i32(1)
    c_mc1 = A.f32(1)
    c_mc2 = A.f32(1)
    mw_acc = A.f32(44)

    PS = [nc.alloc_psum_tensor(f"ps{i}", [128, 512], F32) for i in range(8)]
    kb = KB(nc)

    def rsqrt_row(r_v, r_t0, r_t1, r_out, w, eps=1e-5, iters=4):
        kb.step("ve", lambda e: e.tensor_scalar(r_v[0:1, 0:w], r_v[0:1, 0:w], eps, None, ALU.add))
        kb.step("ve", lambda e: e.tensor_scalar(
            r_t0[0:1, 0:w].bitcast(I32), r_v[0:1, 0:w].bitcast(I32), 1, None, ALU.logical_shift_right))
        kb.step("ve", lambda e: e.tensor_scalar(
            r_t0[0:1, 0:w].bitcast(I32), r_t0[0:1, 0:w].bitcast(I32), 0x5F3759DF, -1,
            ALU.subtract, ALU.mult))
        for _ in range(iters):
            kb.step("ve", lambda e: e.tensor_mul(r_t1[0:1, 0:w], r_t0[0:1, 0:w], r_t0[0:1, 0:w]))
            kb.step("ve", lambda e: e.tensor_mul(r_t1[0:1, 0:w], r_t1[0:1, 0:w], r_v[0:1, 0:w]))
            kb.step("ve", lambda e: e.tensor_scalar(r_t1[0:1, 0:w], r_t1[0:1, 0:w], -0.5, 1.5,
                                                    ALU.mult, ALU.add))
            kb.step("ve", lambda e: e.tensor_mul(r_t0[0:1, 0:w], r_t0[0:1, 0:w], r_t1[0:1, 0:w]))
        kb.step("ve", lambda e: e.tensor_copy(r_out[0:1, 0:w], r_t0[0:1, 0:w]), inc=True)

    kb.on("gp", lambda e: e.load_library(ATTN_LIB))
    for dst, src in ((c_ident_b, ident_b), (c_ident_f, ident_f),
                     (c_zsel, zsel_d), (c_rsel[0:16, :], rsel_d),
                     (c_onescol_b[:, 0:1], onescol_b), (c_onescol_f, onescol_f),
                     (c_onesrow_b[0:1, :], onesrow_b), (c_onesrow_f[0:1, :], onesrow_f),
                     (c_n128, n128col), (c_0125, p0125col), (c_off, p06col)):
        kb.dma("sync", dst, src)
    for j in range(8):
        kb.dma("sync", c_ssel[j], ssel_d[j])
        kb.dma("sync", c_asel[j], asel_d[j])

    A.mark()
    t_pb = A.f32(MAXG)
    t_pbe = A.f32(MAXG)
    t_sum = A.f32(1)
    t_rows = A.f32(128)
    for tau in range(3):
        kb.wait("sync", "tS", kb.led["tS"])
        v = kb.dma("sync", t_pb[:, 0:MAXG], pb[tau])
        kb.wait("sc", "dS", v)
        kb.step("sc", lambda e: e.activation(t_pbe[:, 0:MAXG], t_pb[:, 0:MAXG], AF.Exp), inc=True)
        kb.barrier_from("ve", "tS")
        kb.step("ve", lambda e: e.tensor_reduce(t_sum, t_pbe[:, 0:MAXG], AX.X, ALU.add), inc=True)
        kb.barrier_from("sc", "tV")
        kb.step("sc", lambda e, t=tau: e.activation(c_logB[t], t_sum, AF.Ln), inc=True)

    def col_load(dst_cols, src_ap, nrows, bank):
        kb.wait("sync", "tT", kb.led["tT"])
        v = kb.dma("sync", t_rows[0:nrows, 0:128], src_ap)
        kb.wait("te", "dS", v)
        kb.step("te", lambda e: e.matmul(bank[:, 0:nrows], t_rows[0:nrows, 0:128],
                                         c_ident_f[0:nrows, 0:nrows], start=True, stop=True), inc=True)
        kb.barrier_from("ve", "tT")
        kb.step("ve", lambda e: e.tensor_copy(dst_cols, bank[:, 0:nrows]), inc=True)
        kb.barrier_from("te", "tV")

    col_load(c_bcols[:, 0:32], bqkvg, 32, PS[0])
    col_load(c_boutc[:, 0:8], bout, 8, PS[1])
    col_load(c_cbc[:, 0:32], cb, 32, PS[2])
    col_load(c_cwc[:, 0:96], cw, 96, PS[3])
    kb.step("ve", lambda e: e.tensor_scalar(c_bqs[:, 0:8], c_bcols[:, 0:8], 0.125, None, ALU.mult),
            inc=True)
    kb.step("ve", lambda e: e.tensor_scalar(c_offt[:, :], c_off[:, :], 0.0, None, ALU.add),
            inc=True)
    kb.barrier_from("sc", "tV")
    kb.step("sc", lambda e: e.activation(c_off[:, :], c_offt[:, :], AF.Identity), inc=True)
    kb.barrier_from("ve", "tS")
    kb.step("ve", lambda e: e.tensor_scalar(c_off[:, :], c_off[:, :], -0.5, 128.5,
                                            ALU.mult, ALU.add), inc=True)
    A.release()
    kb.all_sync()

    A.mark()
    wq_sb = A.bf16(8 * 4096)
    c_maskE = A.bf16(EXTN + 2)
    kb.dma("sync", c_maskE[0:1, 0:EXTN], maskE)
    wstage = [A.f32(4096), A.f32(4096)]
    wcast_done = {}
    for k in range(8):
        if k >= 2:
            kb.wait("sc", "tS", wcast_done[k - 2])
        v = kb.dma("sc", wstage[k % 2][:, :], wqkvg[128 * k:128 * k + 128, :])
        kb.wait("sc", "dC", v)
        kb.step("sc", lambda e, k=k: e.activation(wq_sb[:, 4096 * k:4096 * (k + 1)],
                                                  wstage[k % 2][:, :], AF.Identity), inc=True)
        wcast_done[k] = kb.led["tS"]

    _xch0 = A.f32(8 * CH)
    xchs = [_xch0, _xch0]
    x16 = A.bf16(8 * CH)
    xsq = A.bf16(8 * CH)
    ln16 = A.bf16(8 * CH)
    lntmp = A.f32(CH)
    r_m = A.f32(CH)
    r_v = A.f32(CH)
    r_t0 = A.f32(CH)
    r_t1 = A.f32(CH)
    r_A = A.f32(CH)
    r_B = A.f32(CH)
    ev_bufs = [A.bf16(CH) for _ in range(8)]
    kv_bufs = [A.bf16(CH) for _ in range(8)]
    ps_sx, ps_sxx, ps_Ab, ps_Bb, ps_msk = PS[0], PS[1], PS[2], PS[3], PS[4]
    ps_mm = [PS[5], PS[6]]

    prev_end = {"tV": 0, "tS": 0, "tT": 0}
    two_ago = {"tV": 0, "tS": 0}
    spill_led = []
    evac_done = []
    for ci, (a, b) in enumerate(CHUNKS_P1):
        w = b - a
        xch = xchs[ci % 2]
        kb.wait("sync", "tV", prev_end["tV"])
        kb.wait("sync", "tS", prev_end["tS"])
        v = kb.dma("sync", xch[:, 0:8 * w].rearrange("p (g t) -> p g t", g=8),
                   xT.rearrange("(g r) t -> r g t", g=8)[:, :, a:b])
        kb.wait("sc", "dS", v)
        for g in range(8):
            kb.step("sc", lambda e, g=g, w=w: e.activation(x16[:, g * w:(g + 1) * w],
                                                           xch[:, g * w:(g + 1) * w], AF.Identity))
            kb.step("sc", lambda e, g=g, w=w: e.activation(xsq[:, g * w:(g + 1) * w],
                                                           xch[:, g * w:(g + 1) * w], AF.Square),
                    inc=(g == 7))
        kb.barrier_from("te", "tS")
        for g in range(8):
            kb.step("te", lambda e, g=g, w=w: e.matmul(ps_sx[0:1, 0:w], c_onescol_b[:, 0:1],
                                                       x16[:, g * w:(g + 1) * w],
                                                       start=(g == 0), stop=(g == 7)))
        for g in range(8):
            kb.step("te", lambda e, g=g, w=w: e.matmul(ps_sxx[0:1, 0:w], c_onescol_b[:, 0:1],
                                                       xsq[:, g * w:(g + 1) * w],
                                                       start=(g == 0), stop=(g == 7)), inc=(g == 7))
        kb.barrier_from("ve", "tT")
        kb.step("ve", lambda e, w=w: e.tensor_scalar(r_m[0:1, 0:w], ps_sx[0:1, 0:w], 1.0 / D, None, ALU.mult))
        kb.step("ve", lambda e, w=w: e.tensor_scalar(r_v[0:1, 0:w], ps_sxx[0:1, 0:w], 1.0 / D, None, ALU.mult))
        kb.step("ve", lambda e, w=w: e.tensor_mul(r_t0[0:1, 0:w], r_m[0:1, 0:w], r_m[0:1, 0:w]))
        kb.step("ve", lambda e, w=w: e.tensor_sub(r_v[0:1, 0:w], r_v[0:1, 0:w], r_t0[0:1, 0:w]))
        rsqrt_row(r_v, r_t0, r_t1, r_A, w)
        kb.step("ve", lambda e, w=w: e.scalar_tensor_tensor(r_B[0:1, 0:w], r_m[0:1, 0:w], -1.0,
                                                            r_A[0:1, 0:w], ALU.mult, ALU.mult), inc=True)
        kb.barrier_from("te", "tV")
        kb.step("te", lambda e, w=w: e.matmul(ps_Ab[:, 0:w], c_onesrow_f[0:1, :], r_A[0:1, 0:w],
                                              start=True, stop=True))
        kb.step("te", lambda e, w=w: e.matmul(ps_Bb[:, 0:w], c_onesrow_f[0:1, :], r_B[0:1, 0:w],
                                              start=True, stop=True))
        kb.step("te", lambda e, a=a, b=b, w=w: e.matmul(ps_msk[:, 0:w], c_onesrow_b[0:1, :],
                                                        c_maskE[0:1, a:b], start=True, stop=True),
                inc=True)
        kb.barrier_from("ve", "tT")
        for g in range(8):
            kb.step("ve", lambda e, g=g, w=w: e.tensor_mul(lntmp[:, 0:w], xch[:, g * w:(g + 1) * w],
                                                           ps_Ab[:, 0:w]))
            kb.step("ve", lambda e, g=g, w=w: e.tensor_add(ln16[:, g * w:(g + 1) * w], lntmp[:, 0:w],
                                                           ps_Bb[:, 0:w]), inc=(g == 7))
        kb.barrier_from("te", "tV")

        in_ta = a >= TA0
        mlist = list(range(32)) if in_ta else list(range(8, 24))
        ta_a = a - TA0
        for mi, m in enumerate(mlist):
            blk, sub = m // 8, m % 8
            gmi = len(evac_done)
            bank = ps_mm[gmi % 2]
            ev = ev_bufs[gmi % 8]
            kv = kv_bufs[gmi % 8]
            if gmi >= 2:
                kb.wait("te", evac_done[gmi - 2][0], evac_done[gmi - 2][1])
            for k in range(8):
                kb.step("te", lambda e, k=k, m=m, w=w, bank=bank:
                        e.matmul(bank[:, 0:w], wq_sb[:, 4096 * k + 128 * m:4096 * k + 128 * m + 128],
                                 ln16[:, k * w:(k + 1) * w], start=(k == 0), stop=(k == 7)),
                        inc=(k == 7))
            kb.barrier_from("sc", "tT")
            if len(spill_led) >= 8:
                kb.wait("sc", "dG", spill_led[-8])
            if blk == 0:
                kb.step("sc", lambda e, w=w, sub=sub, bank=bank, ev=ev:
                        e.activation(ev[:, 0:w], bank[:, 0:w], AF.Identity,
                                     bias=c_bqs[:, sub:sub + 1], scale=c_0125), inc=True)
                evac_done.append(("tS", kb.led["tS"]))
                kb.barrier_from("gp", "tS")
                spill_led.append(kb.dma("gp", qD[128 * sub:128 * sub + 128, ta_a:ta_a + w], ev[:, 0:w]))
            elif blk == 3:
                kb.step("sc", lambda e, w=w, sub=sub, bank=bank, ev=ev:
                        e.activation(ev[:, 0:w], bank[:, 0:w], AF.Sigmoid,
                                     bias=c_bcols[:, 24 + sub:25 + sub]), inc=True)
                evac_done.append(("tS", kb.led["tS"]))
                kb.barrier_from("gp", "tS")
                spill_led.append(kb.dma("gp", gateD[128 * sub:128 * sub + 128, ta_a:ta_a + w], ev[:, 0:w]))
            else:
                kb.step("sc", lambda e, w=w, m=m, bank=bank, ev=ev:
                        e.activation(ev[:, 0:w], bank[:, 0:w], AF.Identity,
                                     bias=c_bcols[:, m:m + 1]), inc=True)
                kb.barrier_from("ve", "tS")
                if len(spill_led) >= 8:
                    kb.wait("ve", "dG", spill_led[-8])
                kb.step("ve", lambda e, w=w, ev=ev, kv=kv: e.tensor_mul(kv[:, 0:w], ev[:, 0:w],
                                                                        ps_msk[:, 0:w]), inc=True)
                evac_done.append(("tV", kb.led["tV"]))
                kb.barrier_from("gp", "tV")
                dst = kD if blk == 1 else vD
                spill_led.append(kb.dma("gp", dst[128 * sub:128 * sub + 128, a:b], kv[:, 0:w]))
        two_ago = dict(prev_end)
        prev_end = {k2: kb.led[k2] for k2 in ("tV", "tS", "tT")}
    A.release()
    kb.all_sync()

    A.mark()
    mw_st = [A.f32(512), A.f32(512)]
    mw_ab = A.bf16(512)
    kwin = A.bf16(8 * WINC)
    vwin = A.bf16(8 * WINC)
    wo_sb = A.bf16(8 * 1024)
    qch = A.bf16(8 * CH)
    gch = A.bf16(8 * CH)
    Esb = [A.bf16(CH) for _ in range(3)]
    zinv = A.f32(CH)
    zrep = A.f32(CH)
    atn = qch
    A.mark()
    arsb = [A.bf16(CH) for _ in range(NDIST)]
    pbuf = [A.bf16(CH) for _ in range(16)]
    wstage2 = A.f32(1024)
    A.release()
    A.mark()
    xc2 = A.f32(8 * CH)
    x1c = A.f32(8 * CH)
    A.release()
    A._adv(max(NDIST * CH // 2 + 8 * CH // 2 + 1024, 16 * CH))

    for g in range(8):
        kb.barrier_from("sc", "tS")
        v = kb.dma("sc", wstage2[:, :], wout[128 * g:128 * g + 128, :])
        kb.wait("sc", "dC", v)
        kb.step("sc", lambda e, g=g: e.activation(wo_sb[:, 1024 * g:1024 * (g + 1)],
                                                  wstage2[:, :], AF.Identity), inc=True)
    wocast_done = kb.led["tS"]

    def win_pieces(lo, hi):
        out, cur = [], lo
        while cur < hi:
            cend = min((cur // CH + 1) * CH, hi)
            out.append(((cur // CH) % W7 * CH + cur % CH, cend - cur, cur - lo))
            cur = cend
        return out

    mean_state = {"pc": 0, "marks": {}, "grp": 0}
    MEAN_PIECES = []
    for k in range(8):
        for j in range(8):
            MEAN_PIECES.append((w1T[128 * k:128 * k + 128, 512 * j:512 * (j + 1)],))
    for k in range(32):
        for j in range(2):
            MEAN_PIECES.append((w2T[128 * k:128 * k + 128, 512 * j:512 * (j + 1)],))

    def emit_mean_piece():
        st = mean_state
        pc = st["pc"]
        if pc >= len(MEAN_PIECES):
            return False
        (srcap,) = MEAN_PIECES[pc]
        slot = 8 + (pc % 32)
        if pc >= 2:
            kb.wait("sc", "tS", st["marks"][pc - 2])
        v = kb.dma("sc", mw_st[pc % 2][:, :], srcap)
        kb.wait("sc", "dC", v)
        kb.step("sc", lambda e, pc=pc, slot=slot: e.activation(
            mw_ab[:, :], mw_st[pc % 2][:, :], AF.Abs,
            accum_out=mw_acc[:, slot:slot + 1]), inc=True)
        st["marks"][pc] = kb.led["tS"]
        st["pc"] = pc + 1
        if (pc + 1) % 32 == 0:
            g = st["grp"]
            kb.barrier_from("ve", "tS")
            kb.step("ve", lambda e, g=g: e.tensor_reduce(mw_acc[:, g:g + 1], mw_acc[:, 8:40],
                                                         AX.X, ALU.add), inc=True)
            kb.barrier_from("sc", "tV")
            st["grp"] = g + 1
        return True

    spill_done = kb.led["dG"]
    kb.wait("sync", "dG", spill_done)
    for ci in range(6):
        a, b = CHUNKS_E[ci]
        slot = ci % W7
        for g in range(8):
            kb.dma("sync", kwin[:, g * WINC + slot * CH: g * WINC + slot * CH + (b - a)],
                   kD[128 * g:128 * g + 128, a:b])
            kb.dma("sync", vwin[:, g * WINC + slot * CH: g * WINC + slot * CH + (b - a)],
                   vD[128 * g:128 * g + 128, a:b])

    ps_S = [PS[0], PS[1], PS[2]]
    ps_Z, ps_ar, ps_av, ps_op = PS[3], PS[4], PS[5], PS[6]

    prev_end = {"tV": 0, "tS": 0}
    x1_spill = 0
    for ct, (a, b) in enumerate(CHUNKS_TA):
        w = b - a
        ta_a = a - TA0
        ci = 6 + ct
        ca_, cb_ = CHUNKS_E[ci]
        slot = ci % W7
        kb.wait("sync", "tV", prev_end["tV"])
        kb.wait("te", "tV", prev_end["tV"])
        for g in range(8):
            kb.dma("sync", kwin[:, g * WINC + slot * CH: g * WINC + slot * CH + (cb_ - ca_)],
                   kD[128 * g:128 * g + 128, ca_:cb_])
            kb.dma("sync", vwin[:, g * WINC + slot * CH: g * WINC + slot * CH + (cb_ - ca_)],
                   vD[128 * g:128 * g + 128, ca_:cb_])
        kb.dma("sync", qch[:, 0:8 * w].rearrange("p (g t) -> p g t", g=8),
               qD.rearrange("(g r) t -> r g t", g=8)[:, :, ta_a:ta_a + w])
        v = kb.dma("sync", gch[:, 0:8 * w].rearrange("p (g t) -> p g t", g=8),
                   gateD.rearrange("(g r) t -> r g t", g=8)[:, :, ta_a:ta_a + w])
        kb.wait("ve", "dS", v)
        kb.wait("te", "dS", v)
        if ct == 0:
            kb.wait("ve", "tS", wocast_done)
        else:
            kb.wait("ve", "dG", x1_spill)
            kb.wait("sc", "dG", x1_spill)

        te_di = {}
        for di in range(NDIST):
            tau, j, d = di // 8, di % 8, DIST[di]
            pieces = win_pieces(a - d, b - d)
            if di >= 2:
                kb.wait("ve", "tT", te_di[di - 2])
            sset = (di % 2) * 8
            for g in range(8):
                pb_ = pbuf[sset + g]
                for pi_, (wofs, ln_, ofs) in enumerate(pieces):
                    last = (g == 7 and pi_ == len(pieces) - 1)
                    kb.step("ve", lambda e, pb_=pb_, g=g, wofs=wofs, ln_=ln_, ofs=ofs, w=w:
                            e.tensor_mul(pb_[:, ofs:ofs + ln_],
                                         qch[:, g * w + ofs:g * w + ofs + ln_],
                                         kwin[:, g * WINC + wofs:g * WINC + wofs + ln_]), inc=last)
            kb.wait("te", "tV", kb.led["tV"])
            for g in range(8):
                st = (di % 8 == 0 and g == 0)
                sp = (di % 8 == 7 and g == 7)
                kb.step("te", lambda e, tau=tau, j=j, g=g, w=w, st=st, sp=sp, sset=sset:
                        e.matmul(ps_S[tau][:, 0:w], c_ssel[j], pbuf[sset + g][:, 0:w],
                                 start=st, stop=sp), inc=(g == 7))
            te_di[di] = kb.led["tT"]

        kb.barrier_from("sc", "tT")
        for tau in range(3):
            kb.step("sc", lambda e, tau=tau, w=w: e.activation(Esb[tau][:, 0:w], ps_S[tau][:, 0:w],
                                                               AF.Exp, bias=c_logB[tau]), inc=(tau == 2))
        kb.barrier_from("te", "tS")
        for tau in range(3):
            kb.step("te", lambda e, tau=tau, w=w: e.matmul(ps_Z[0:16, 0:w], c_zsel, Esb[tau][:, 0:w],
                                                           start=(tau == 0), stop=(tau == 2)),
                    inc=(tau == 2))
        kb.barrier_from("ve", "tT")
        kb.step("ve", lambda e, w=w: e.reciprocal(zinv[0:16, 0:w], ps_Z[0:16, 0:w]), inc=True)
        kb.barrier_from("te", "tV")
        kb.step("te", lambda e, w=w: e.matmul(ps_ar[:, 0:w], c_rsel[0:16, :], zinv[0:16, 0:w],
                                              start=True, stop=True), inc=True)
        kb.barrier_from("sc", "tT")
        kb.step("sc", lambda e, w=w: e.activation(zrep[:, 0:w], ps_ar[:, 0:w], AF.Identity), inc=True)
        kb.barrier_from("te", "tS")
        kb.barrier_from("sc", "tV")
        ar_sc = {}
        for di in range(NDIST):
            tau, j = di // 8, di % 8
            arbank = ps_ar if di % 2 == 0 else ps_av
            if di >= 2:
                kb.wait("te", "tS", ar_sc[di - 2])
            tt = kb.step("te", lambda e, tau=tau, j=j, w=w, arbank=arbank:
                         e.matmul(arbank[:, 0:w], c_asel[j], Esb[tau][:, 0:w], start=True, stop=True),
                         inc=True)
            kb.wait("sc", "tT", tt)
            ts = kb.step("sc", lambda e, di=di, w=w, arbank=arbank: e.activation(
                arsb[di][:, 0:w], arbank[:, 0:w], AF.Identity), inc=True)
            ar_sc[di] = ts
        kb.barrier_from("ve", "tS")
        av_evac = {}
        for g in range(8):
            te_mark = {}
            avbank = ps_av if g % 2 == 0 else ps_ar
            if g >= 2:
                kb.wait("te", "tV", av_evac[g - 2])
            for di in range(NDIST):
                d = DIST[di]
                pieces = win_pieces(a - d, b - d)
                pb_ = pbuf[8 + di % 4]
                if di >= 4:
                    kb.wait("ve", "tT", te_mark[di - 4])
                for pi_, (wofs, ln_, ofs) in enumerate(pieces):
                    last = (pi_ == len(pieces) - 1)
                    kb.step("ve", lambda e, pb_=pb_, g=g, di=di, wofs=wofs, ln_=ln_, ofs=ofs:
                            e.tensor_mul(pb_[:, ofs:ofs + ln_], arsb[di][:, ofs:ofs + ln_],
                                         vwin[:, g * WINC + wofs:g * WINC + wofs + ln_]), inc=last)
                kb.wait("te", "tV", kb.led["tV"])
                kb.step("te", lambda e, pb_=pb_, w=w, st=(di == 0), sp=(di == NDIST - 1), avbank=avbank:
                        e.matmul(avbank[:, 0:w], c_ident_b, pb_[:, 0:w], start=st, stop=sp), inc=True)
                te_mark[di] = kb.led["tT"]
            kb.barrier_from("ve", "tT")
            kb.step("ve", lambda e, g=g, w=w, avbank=avbank: e.tensor_mul(atn[:, g * w:(g + 1) * w],
                                                                          avbank[:, 0:w],
                                                                          zrep[:, 0:w]), inc=True)
            av_evac[g] = kb.led["tV"]
        for g in range(8):
            kb.step("ve", lambda e, g=g, w=w: e.tensor_mul(atn[:, g * w:(g + 1) * w],
                                                           atn[:, g * w:(g + 1) * w],
                                                           gch[:, g * w:(g + 1) * w]), inc=(g == 7))
        kb.barrier_from("te", "tV")
        kb.barrier_from("sync", "tV")
        v = kb.dma("sync", xc2[:, 0:8 * w].rearrange("p (g t) -> p g t", g=8),
                   xT.rearrange("(g r) t -> r g t", g=8)[:, :, a:b])
        kb.wait("ve", "dS", v)
        op_evac = {}
        for m in range(8):
            opbank = ps_op if m % 2 == 0 else ps_Z
            if m >= 2:
                kb.wait("te", "tV", op_evac[m - 2])
            for g in range(8):
                kb.step("te", lambda e, m=m, g=g, w=w, opbank=opbank:
                        e.matmul(opbank[:, 0:w], wo_sb[:, 1024 * g + 128 * m:1024 * g + 128 * m + 128],
                                 atn[:, g * w:(g + 1) * w], start=(g == 0), stop=(g == 7)),
                        inc=(g == 7))
            kb.barrier_from("ve", "tT")
            kb.step("ve", lambda e, m=m, w=w, opbank=opbank: e.scalar_tensor_tensor(
                x1c[:, m * w:(m + 1) * w], opbank[:, 0:w], c_boutc[:, m:m + 1],
                xc2[:, m * w:(m + 1) * w], ALU.add, ALU.add), inc=True)
            op_evac[m] = kb.led["tV"]
        kb.barrier_from("gp", "tV")
        for g in range(8):
            x1_spill = kb.dma("gp", x1D[128 * g:128 * g + 128, ta_a:ta_a + w],
                              x1c[:, g * w:(g + 1) * w])
        for _ in range(26):
            emit_mean_piece()
        prev_end = {"tV": kb.led["tV"], "tS": kb.led["tS"]}
    while emit_mean_piece():
        pass
    A.release()
    kb.all_sync()

    A.mark()
    tern1 = A.bf16(8 * 4096)
    A.mark()
    wst = [A.f32(4096) for _ in range(4)]
    qtmps = [A.i32(4096), A.i32(4096)]
    t2tiles = [A.bf16(1024), A.bf16(1024)]

    def finalize_mean(g0, g1, n, mc_dst, sw_slot):
        kb.step("ve", lambda e: e.tensor_add(mw_acc[:, 6:7], mw_acc[:, g0:g0 + 1],
                                             mw_acc[:, g1:g1 + 1]), inc=True)
        kb.barrier_from("te", "tV")
        kb.step("te", lambda e: e.matmul(PS[7][0:1, 0:1], c_onescol_f, mw_acc[:, 6:7],
                                         start=True, stop=True), inc=True)
        kb.barrier_from("ve", "tT")
        kb.step("ve", lambda e, n=n: e.tensor_scalar(mw_acc[0:1, 7:8], PS[7][0:1, 0:1],
                                                     1.0 / n, 1e-5, ALU.mult, ALU.max), inc=True)
        kb.barrier_from("te", "tV")
        kb.step("te", lambda e: e.matmul(PS[7][:, 0:1], c_onesrow_f[0:1, :], mw_acc[0:1, 7:8],
                                         start=True, stop=True), inc=True)
        kb.barrier_from("ve", "tT")
        kb.step("ve", lambda e: e.tensor_copy(mc_dst, PS[7][:, 0:1]))
        kb.step("ve", lambda e, s=sw_slot: e.reciprocal(mw_acc[:, s:s + 1], PS[7][:, 0:1]), inc=True)
        kb.barrier_from("ve", "tV")

    def quant_pass(wT, ntiles, fdim, tern_dst, t2_spill, sw_slot):
        ve_mark = {}
        sc_mark = {}
        dma_led = {}
        kb.barrier_from("sync", "tV")
        for k in range(min(4, ntiles)):
            dma_led[k] = kb.dma("sync", wst[k % 4][:, 0:fdim], wT[128 * k:128 * k + 128, :])
        for k in range(ntiles):
            kb.wait("ve", "dS", dma_led[k])
            if k >= 2:
                kb.wait("ve", "tS", sc_mark[k - 2])
            qt = qtmps[k % 2]
            kb.step("ve", lambda e, fdim=fdim, k=k, qt=qt, s=sw_slot: e.tensor_scalar(
                qt[:, 0:fdim], wst[k % 4][:, 0:fdim], mw_acc[:, s:s + 1], c_off[:, 0:1],
                ALU.mult, ALU.add))
            kb.step("ve", lambda e, fdim=fdim, qt=qt: e.tensor_scalar(
                qt[:, 0:fdim], qt[:, 0:fdim], 129, 127, ALU.min, ALU.max), inc=True)
            ve_mark[k] = kb.led["tV"]
            kb.barrier_from("sc", "tV")
            if tern_dst is not None:
                kb.step("sc", lambda e, k=k, fdim=fdim, qt=qt: e.activation(
                    tern_dst[:, fdim * k:fdim * (k + 1)], qt[:, 0:fdim], AF.Identity,
                    bias=c_n128), inc=True)
            else:
                t2t = t2tiles[k % 2]
                if k >= 2:
                    kb.wait("sc", "dG", spill_led2[k - 2])
                kb.step("sc", lambda e, fdim=fdim, t2t=t2t, qt=qt: e.activation(
                    t2t[:, 0:fdim], qt[:, 0:fdim], AF.Identity, bias=c_n128), inc=True)
                kb.barrier_from("gp", "tS")
                spill_led2.append(kb.dma("gp", t2_spill[128 * k:128 * k + 128, :], t2t[:, 0:fdim]))
            sc_mark[k] = kb.led["tS"]
            if k + 4 < ntiles:
                kb.wait("sync", "tV", ve_mark[k])
                dma_led[k + 4] = kb.dma("sync", wst[k % 4][:, 0:fdim],
                                        wT[128 * (k + 4):128 * (k + 4) + 128, :])

    spill_led2 = []
    finalize_mean(0, 1, 8 * 512 * 128 * 8, c_mc1, 40)
    finalize_mean(2, 3, 64 * 512 * 128, c_mc2, 41)
    quant_pass(w1T, 8, 4096, tern1, None, 40)
    quant_pass(w2T, 32, 1024, None, t2D, 41)
    A.release()
    kb.all_sync()

    A.mark()
    m2b = A.f32(CH)
    XS = CH + 2
    x1p = A.f32(8 * XS)
    ln2t = A.f32(CH)
    sqb = ln2t
    xq1 = A.bf16(8 * 352)
    Mt = A.f32(CH)
    srepT = A.f32(CH)
    rr1T = A.f32(CH)
    AsT = A.f32(CH)
    BsT = A.f32(CH)
    SILW = 352
    silu = A.f32(32 * SILW)
    xq2x = A.bf16(2 * CH)
    hbufs = [A.f32(CH + 2), A.f32(CH + 2)]
    ybufs = [A.f32(CH), A.f32(CH)]
    htail = A.f32(64)
    qi32s = [A.i32(CH), A.i32(CH)]
    t2st = [A.bf16(1024), A.bf16(1024)]
    x2bufs = [A.f32(CH) for _ in range(2)]
    asb = [A.f32(CH), A.f32(CH)]
    r_m3 = A.f32(CH)
    r_v3 = A.f32(CH)
    r_t03 = A.f32(CH)
    r_t13 = A.f32(CH)
    r_A3 = A.f32(CH)
    r_B3 = A.f32(CH)
    U1, U0 = ybufs[0], ybufs[1]
    s2T, rr2T = AsT, BsT

    def xq2_ap(m, wc):
        if m < 2:
            return xq2x[:, m * CH:m * CH + wc]
        base = (m - 2) * SILW
        return silu[:, base:base + SILW].bitcast(BF16)[:, 0:wc]

    prev_end3 = {"tV": 0, "tS": 0, "tT": 0}
    out_spills = []
    for i3, (fa, fb, ca3, cb3) in enumerate(P3_RANGES):
        w3 = fb - fa
        wc = cb3 - ca3
        kb.wait("sync", "tV", prev_end3["tV"])
        kb.wait("sync", "tS", prev_end3["tS"])
        kb.wait("te", "tV", prev_end3["tV"])
        kb.wait("gp", "tV", prev_end3["tV"])
        kb.wait("gp", "tS", prev_end3["tS"])
        vmask = kb.dma("sync", m2b[0:1, 0:w3], mask2[0:1, fa:fa + w3])
        x1_lo = min(fa, ca3)
        xo = fa - x1_lo
        w3x = fb - x1_lo
        v = kb.dma("sync", x1p.rearrange("p (g t) -> p g t", t=XS)[:, :, 0:w3x],
                   x1D.rearrange("(g r) t -> r g t", g=8)[:, :, x1_lo:fb])
        kb.wait("sc", "dS", v)
        kb.wait("te", "dS", v)
        kb.wait("ve", "dS", v)
        kb.wait("ve", "tS", prev_end3["tS"])
        for g in range(8):
            xg = lambda g=g: x1p[:, g * XS + xo:g * XS + xo + w3]
            if g == 0:
                kb.step("ve", lambda e, xg=xg, w3=w3: e.tensor_copy(U1[:, 0:w3], xg()))
                kb.step("ve", lambda e, xg=xg, w3=w3: e.tensor_scalar(
                    U0[:, 0:w3], xg(), -1.0, None, ALU.mult))
            else:
                kb.step("ve", lambda e, xg=xg, w3=w3: e.tensor_max(U1[:, 0:w3], U1[:, 0:w3], xg()))
                kb.step("ve", lambda e, xg=xg, w3=w3: e.scalar_tensor_tensor(
                    U0[:, 0:w3], xg(), -1.0, U0[:, 0:w3], ALU.mult, ALU.max), inc=(g == 7))
        kb.barrier_from("gp", "tV")
        kb.step("gp", lambda e, w3=w3: e.partition_all_reduce(
            U1[:, 0:w3], U1[:, 0:w3], 128, ReduceOp.max), inc=True)
        kb.barrier_from("gp", "tG")
        kb.step("gp", lambda e, w3=w3: e.partition_all_reduce(
            U0[:, 0:w3], U0[:, 0:w3], 128, ReduceOp.max), inc=True)
        par_led = kb.led["tG"]
        for g in range(8):
            kb.step("te", lambda e, g=g, w3=w3, xo=xo: e.matmul(PS[0][0:1, 0:w3], c_onescol_f,
                                                         x1p[:, g * XS + xo:g * XS + xo + w3],
                                                         start=(g == 0), stop=(g == 7)), inc=(g == 7))
        for g in range(8):
            kb.step("sc", lambda e, g=g, w3=w3, xo=xo: e.activation(
                sqb[:, 0:w3], x1p[:, g * XS + xo:g * XS + xo + w3], AF.Square), inc=True)
            kb.barrier_from("te", "tS")
            kb.step("te", lambda e, g=g, w3=w3: e.matmul(PS[1][0:1, 0:w3], c_onescol_f, sqb[:, 0:w3],
                                                         start=(g == 0), stop=(g == 7)), inc=True)
            if g < 7:
                kb.barrier_from("sc", "tT")
        kb.barrier_from("ve", "tT")
        kb.step("ve", lambda e, w3=w3: e.tensor_scalar(r_m3[0:1, 0:w3], PS[0][0:1, 0:w3], 1.0 / D,
                                                       None, ALU.mult))
        kb.step("ve", lambda e, w3=w3: e.tensor_scalar(r_v3[0:1, 0:w3], PS[1][0:1, 0:w3], 1.0 / D,
                                                       None, ALU.mult))
        kb.step("ve", lambda e, w3=w3: e.tensor_mul(r_t03[0:1, 0:w3], r_m3[0:1, 0:w3], r_m3[0:1, 0:w3]))
        kb.step("ve", lambda e, w3=w3: e.tensor_sub(r_v3[0:1, 0:w3], r_v3[0:1, 0:w3], r_t03[0:1, 0:w3]))
        rsqrt_row(r_v3, r_t03, r_t13, r_A3, w3)
        kb.step("ve", lambda e, w3=w3: e.scalar_tensor_tensor(r_B3[0:1, 0:w3], r_m3[0:1, 0:w3], -1.0,
                                                              r_A3[0:1, 0:w3], ALU.mult, ALU.mult),
                inc=True)
        kb.barrier_from("te", "tV")
        kb.step("te", lambda e, w3=w3: e.matmul(PS[2][:, 0:w3], c_onesrow_f[0:1, :], r_A3[0:1, 0:w3],
                                                start=True, stop=True))
        kb.step("te", lambda e, w3=w3: e.matmul(PS[3][:, 0:w3], c_onesrow_f[0:1, :], r_B3[0:1, 0:w3],
                                                start=True, stop=True))
        kb.wait("te", "dS", vmask)
        kb.step("te", lambda e, w3=w3: e.matmul(PS[4][:, 0:w3], c_onesrow_f[0:1, :], m2b[0:1, 0:w3],
                                                start=True, stop=True), inc=True)
        kb.barrier_from("ve", "tT")
        kb.wait("ve", "tG", par_led)
        kb.step("ve", lambda e, w3=w3: e.tensor_mul(ln2t[:, 0:w3], U1[:, 0:w3], PS[2][:, 0:w3]))
        kb.step("ve", lambda e, w3=w3: e.tensor_add(U1[:, 0:w3], ln2t[:, 0:w3], PS[3][:, 0:w3]))
        kb.step("ve", lambda e, w3=w3: e.tensor_mul(ln2t[:, 0:w3], U0[:, 0:w3], PS[2][:, 0:w3]))
        kb.step("ve", lambda e, w3=w3: e.tensor_sub(U0[:, 0:w3], ln2t[:, 0:w3], PS[3][:, 0:w3]))
        kb.step("ve", lambda e, w3=w3: e.tensor_max(U1[:, 0:w3], U1[:, 0:w3], U0[:, 0:w3]))
        kb.step("ve", lambda e, w3=w3: e.tensor_scalar(U1[:, 0:w3], U1[:, 0:w3], 1e-5, None, ALU.max))
        kb.step("ve", lambda e, w3=w3: e.reciprocal(srepT[:, 0:w3], U1[:, 0:w3]))
        kb.step("ve", lambda e, w3=w3: e.tensor_scalar(srepT[:, 0:w3], srepT[:, 0:w3], 127.0,
                                                       None, ALU.mult))
        kb.step("ve", lambda e, w3=w3: e.tensor_mul(srepT[:, 0:w3], srepT[:, 0:w3], PS[4][:, 0:w3]))
        kb.step("ve", lambda e, w3=w3: e.tensor_scalar(rr1T[:, 0:w3], U1[:, 0:w3], c_mc1[:, 0:1],
                                                       1.0 / 127.0, ALU.mult, ALU.mult))
        kb.step("ve", lambda e, w3=w3: e.tensor_mul(AsT[:, 0:w3], srepT[:, 0:w3], PS[2][:, 0:w3]))
        kb.step("ve", lambda e, w3=w3: e.tensor_mul(BsT[:, 0:w3], srepT[:, 0:w3], PS[3][:, 0:w3]))
        kb.step("ve", lambda e, w3=w3: e.tensor_scalar(BsT[:, 0:w3], BsT[:, 0:w3], c_off[:, 0:1],
                                                       None, ALU.add), inc=True)
        sc_marks = {}
        for g in range(8):
            qi = qi32s[g % 2]
            if g >= 2:
                kb.wait("ve", "tS", sc_marks[g - 2])
            kb.step("ve", lambda e, g=g, w3=w3, xo=xo: e.tensor_mul(
                ln2t[:, 0:w3], x1p[:, g * XS + xo:g * XS + xo + w3], AsT[:, 0:w3]))
            kb.step("ve", lambda e, g=g, w3=w3, qi=qi: e.tensor_add(qi[:, 0:w3], ln2t[:, 0:w3],
                                                                    BsT[:, 0:w3]), inc=True)
            kb.barrier_from("sc", "tV")
            kb.step("sc", lambda e, g=g, w3=w3, qi=qi: e.activation(xq1[:, g * w3:(g + 1) * w3],
                                                                    qi[:, 0:w3], AF.Identity,
                                                                    bias=c_n128), inc=True)
            sc_marks[g] = kb.led["tS"]
        kb.barrier_from("te", "tS")
        kb.barrier_from("ve", "tS")
        sil_marks = {}
        abs_marks = {}
        max_marks = {}
        ybm_marks = {}
        h_done = {}
        lo = ca3 - 1 - fa + 2
        for m in range(32):
            bank = PS[6] if m % 2 == 0 else PS[7]
            yb = ybufs[m % 2]
            hb = hbufs[m % 2]
            if m >= 2:
                kb.wait("te", "tV", h_done[m - 2])
            for k in range(8):
                kb.step("te", lambda e, k=k, m=m, w3=w3, bank=bank:
                        e.matmul(bank[:, 0:w3],
                                 tern1[:, 4096 * k + 128 * m:4096 * k + 128 * m + 128],
                                 xq1[:, k * w3:(k + 1) * w3], start=(k == 0), stop=(k == 7)),
                        inc=(k == 7))
            kb.barrier_from("ve", "tT")
            if i3 > 0:
                kb.step("ve", lambda e, m=m, hb=hb: e.tensor_copy(hb[:, 0:2],
                                                                  htail[:, 2 * m:2 * m + 2]))
            if m >= 2:
                kb.wait("ve", "tS", ybm_marks[m - 2])
            kb.step("ve", lambda e, m=m, w3=w3, bank=bank, hb=hb: e.tensor_mul(
                hb[:, 2:2 + w3], bank[:, 0:w3], rr1T[:, 0:w3]), inc=True)
            h_done[m] = kb.led["tV"]
            kb.barrier_from("sc", "tV")
            kb.step("sc", lambda e, m=m, lo=lo, wc=wc, yb=yb, hb=hb: e.activation(
                yb[:, 0:wc], hb[:, lo:lo + wc], AF.Identity,
                bias=c_cbc[:, m:m + 1], scale=c_cwc[:, m:m + 1]), inc=True)
            ybm_marks[m] = kb.led["tS"]
            if m >= 1:
                kb.wait("sc", "tS", sil_marks[m - 1])
                if m >= 3:
                    kb.wait("sc", "tV", max_marks[m - 3])
                kb.step("sc", lambda e, m=m, wc=wc: e.activation(
                    asb[(m - 1) % 2][:, 0:wc], silu[:, (m - 1) * SILW:(m - 1) * SILW + wc],
                    AF.Abs), inc=True)
                abs_marks[m - 1] = kb.led["tS"]
            if m >= 1:
                kb.wait("ve", "tS", abs_marks[m - 1])
                if m == 1:
                    kb.step("ve", lambda e, wc=wc: e.tensor_copy(Mt[:, 0:wc], asb[0][:, 0:wc]),
                            inc=True)
                else:
                    kb.step("ve", lambda e, m=m, wc=wc: e.tensor_max(
                        Mt[:, 0:wc], Mt[:, 0:wc], asb[(m - 1) % 2][:, 0:wc]), inc=True)
                max_marks[m - 1] = kb.led["tV"]
            kb.wait("ve", "tS", ybm_marks[m])
            kb.step("ve", lambda e, m=m, lo=lo, wc=wc, yb=yb, hb=hb: e.scalar_tensor_tensor(
                yb[:, 0:wc], hb[:, lo + 1:lo + 1 + wc], c_cwc[:, 32 + m:33 + m], yb[:, 0:wc],
                ALU.mult, ALU.add))
            kb.step("ve", lambda e, m=m, lo=lo, wc=wc, yb=yb, hb=hb: e.scalar_tensor_tensor(
                yb[:, 0:wc], hb[:, lo + 2:lo + 2 + wc], c_cwc[:, 64 + m:65 + m], yb[:, 0:wc],
                ALU.mult, ALU.add), inc=True)
            kb.step("ve", lambda e, m=m, w3=w3, hb=hb: e.tensor_copy(htail[:, 2 * m:2 * m + 2],
                                                                     hb[:, w3:w3 + 2]))
            kb.barrier_from("sc", "tV")
            kb.step("sc", lambda e, m=m, wc=wc, yb=yb: e.activation(silu[:, m * SILW:m * SILW + wc],
                                                                    yb[:, 0:wc], AF.Silu), inc=True)
            sil_marks[m] = kb.led["tS"]
        kb.wait("sc", "tS", sil_marks[31])
        kb.step("sc", lambda e, wc=wc: e.activation(asb[31 % 2][:, 0:wc],
                                                    silu[:, 31 * SILW:31 * SILW + wc], AF.Abs),
                inc=True)
        abs_marks[31] = kb.led["tS"]
        kb.wait("ve", "tS", abs_marks[31])
        kb.step("ve", lambda e, wc=wc: e.tensor_max(Mt[:, 0:wc], Mt[:, 0:wc],
                                                    asb[31 % 2][:, 0:wc]), inc=True)
        kb.barrier_from("gp", "tV")
        kb.step("gp", lambda e, wc=wc: e.partition_all_reduce(
            Mt[:, 0:wc], Mt[:, 0:wc], 128, ReduceOp.absmax), inc=True)
        kb.wait("ve", "tG", kb.led["tG"])
        kb.step("ve", lambda e, wc=wc: e.tensor_scalar(Mt[:, 0:wc], Mt[:, 0:wc], 1e-5, None, ALU.max))
        kb.step("ve", lambda e, wc=wc: e.reciprocal(s2T[:, 0:wc], Mt[:, 0:wc]))
        kb.step("ve", lambda e, wc=wc: e.tensor_scalar(s2T[:, 0:wc], s2T[:, 0:wc], 127.0,
                                                       None, ALU.mult))
        kb.step("ve", lambda e, wc=wc: e.tensor_scalar(rr2T[:, 0:wc], Mt[:, 0:wc], c_mc2[:, 0:1],
                                                       1.0 / 127.0, ALU.mult, ALU.mult), inc=True)
        sc_marks = {}
        te_k = {}
        for k in range(32):
            qi = qi32s[k % 2]
            if k >= 2:
                kb.wait("ve", "tS", sc_marks[k - 2])
            kb.step("ve", lambda e, k=k, wc=wc: e.tensor_mul(ln2t[:, 0:wc],
                                                             silu[:, k * SILW:k * SILW + wc], s2T[:, 0:wc]))
            kb.step("ve", lambda e, k=k, wc=wc, qi=qi: e.tensor_scalar(qi[:, 0:wc], ln2t[:, 0:wc],
                                                                       c_off[:, 0:1], None, ALU.add), inc=True)
            kb.barrier_from("sc", "tV")
            kb.step("sc", lambda e, k=k, wc=wc, qi=qi: e.activation(xq2_ap(k, wc), qi[:, 0:wc],
                                                                    AF.Identity, bias=c_n128), inc=True)
            sc_marks[k] = kb.led["tS"]
            if k >= 2:
                kb.wait("sync", "tT", te_k[k - 2])
            v = kb.dma("sync", t2st[k % 2][:, :], t2D[128 * k:128 * k + 128, :])
            kb.wait("te", "dS", v)
            kb.wait("te", "tS", sc_marks[k])
            for m2 in range(8):
                kb.step("te", lambda e, k=k, m2=m2, wc=wc:
                        e.matmul(PS[m2][:, 0:wc], t2st[k % 2][:, 128 * m2:128 * m2 + 128],
                                 xq2_ap(k, wc), start=(k == 0), stop=(k == 31)), inc=(m2 == 7))
            te_k[k] = kb.led["tT"]
        kb.barrier_from("ve", "tT")
        for m2 in range(8):
            x2b = x2bufs[m2 % 2]
            if len(out_spills) >= 2:
                kb.wait("ve", "dG", out_spills[-2])
            kb.step("ve", lambda e, m2=m2, wc=wc, x2b=x2b: e.tensor_mul(x2b[:, 0:wc], PS[m2][:, 0:wc],
                                                                        rr2T[:, 0:wc]))
            kb.step("ve", lambda e, m2=m2, wc=wc, co=ca3 - x1_lo, x2b=x2b: e.tensor_add(
                x2b[:, 0:wc], x2b[:, 0:wc],
                x1p[:, m2 * XS + co:m2 * XS + co + wc]), inc=True)
            kb.barrier_from("gp", "tV")
            out_spills.append(kb.dma("gp", out_xT[128 * m2:128 * m2 + 128, ca3 - HF:cb3 - HF],
                                     x2b[:, 0:wc]))
        prev_end3 = {"tV": kb.led["tV"], "tS": kb.led["tS"], "tT": kb.led["tT"]}
    A.release()
    kb.all_sync()

    import contextlib
    with contextlib.ExitStack() as es:
        block = es.enter_context(nc.Block())

        def mk(engname):
            def body(e):
                for eng, fn in kb.ops:
                    if eng == engname:
                        fn(e)
            return body

        block.sync(mk("sync"))
        block.gpsimd(mk("gp"))
        block.tensor(mk("te"))
        block.vector(mk("ve"))
        block.scalar(mk("sc"))
    lower_extended_insts(nc)
    return nc


def make_consts(w_qkv, b_qkv, w_out, b_out, w_gate, b_gate, pos_bias, w1,
                conv_w, conv_b, w2):
    consts = {}
    consts["wqkvg"] = np.ascontiguousarray(np.concatenate(
        [w_qkv[:, 0:D][:, PERM], w_qkv[:, D:2 * D][:, PERM],
         w_qkv[:, 2 * D:][:, PERM], w_gate[:, PERM]], axis=1))
    consts["bqkvg"] = np.ascontiguousarray(np.concatenate(
        [b_qkv[0:D][PERM], b_qkv[D:2 * D][PERM], b_qkv[2 * D:][PERM],
         b_gate[PERM]]).reshape(32, 128))
    consts["wout"] = np.ascontiguousarray(w_out[PERM, :])
    consts["bout"] = np.ascontiguousarray(b_out.reshape(8, 128))
    consts["w1T"] = np.ascontiguousarray(w1.T)
    consts["w2T"] = np.ascontiguousarray(w2.T)
    consts["cw"] = np.ascontiguousarray(conv_w[:, 0, :].T.reshape(3, 32, 128).reshape(96, 128))
    consts["cb"] = np.ascontiguousarray(conv_b.reshape(32, 128))
    pbb = np.full((3, 128, MAXG), -1e30, np.float32)
    for di in range(NDIST):
        tau, j = di // 8, di % 8
        for i, src in enumerate(GROUPS[di]):
            for h in range(H):
                pbb[tau, 16 * j + h, i] = pos_bias[src, h]
    consts["pb"] = pbb
    consts["ident_bf16"] = np.eye(128, dtype=ml_dtypes.bfloat16)
    consts["ident_f32"] = np.eye(128, dtype=np.float32)
    ssel = np.zeros((8, 128, 128), np.float32)
    asel = np.zeros((8, 128, 128), np.float32)
    for j in range(8):
        for p in range(128):
            ssel[j, p, 16 * j + p // 8] = 1.0
            asel[j, 16 * j + p // 8, p] = 1.0
    consts["ssel"] = ssel.astype(ml_dtypes.bfloat16)
    consts["asel"] = asel.astype(ml_dtypes.bfloat16)
    zsel = np.zeros((128, 16), np.float32)
    for p in range(128):
        zsel[p, p % 16] = 1.0
    consts["zsel"] = zsel.astype(ml_dtypes.bfloat16)
    rsel = np.zeros((16, 128), np.float32)
    for p in range(128):
        rsel[p // 8, p] = 1.0
    consts["rsel"] = rsel
    consts["ones_col_bf16"] = np.ones((128, 1), ml_dtypes.bfloat16)
    consts["ones_col_f32"] = np.ones((128, 1), np.float32)
    consts["ones_row_bf16"] = np.ones((1, 128), ml_dtypes.bfloat16)
    consts["ones_row_f32"] = np.ones((1, 128), np.float32)
    consts["n128_col"] = np.full((128, 1), -128.0, np.float32)
    consts["p0125_col"] = np.full((128, 1), 0.125, np.float32)
    consts["p06_col"] = np.full((128, 1), 0.6, np.float32)
    return consts


def stage_core(c, xF, consts):
    t0 = c * U
    ext_lo = t0 - HK
    xs = np.zeros((D, EXTN), np.float32)
    lo, hi = max(ext_lo, 0), min(ext_lo + EXTN, NSEQ)
    xs[:, lo - ext_lo: hi - ext_lo] = xF[:, lo:hi]
    idx = np.arange(ext_lo, ext_lo + EXTN)
    mE = ((idx >= 0) & (idx < NSEQ)).astype(ml_dtypes.bfloat16)[None, :]
    idx2 = np.arange(t0 - HF, t0 + U + HF)
    m2 = ((idx2 >= 0) & (idx2 < NSEQ)).astype(np.float32)[None, :]
    d = dict(consts)
    d["xT"] = xs
    d["maskE"] = np.ascontiguousarray(mE)
    d["mask2"] = np.ascontiguousarray(m2)
    return d


def kernel(L0_attn_x, L0_attn_w_qkv, L0_attn_b_qkv, L0_attn_w_out, L0_attn_b_out,
           L0_attn_w_gate, L0_attn_b_gate, L0_attn_pos_bias, L0_ffn_w1,
           L0_ffn_conv_w, L0_ffn_conv_b, L0_ffn_w2):
    xF = np.ascontiguousarray(np.asarray(L0_attn_x, np.float32)[0].T)
    consts = make_consts(np.asarray(L0_attn_w_qkv, np.float32),
                         np.asarray(L0_attn_b_qkv, np.float32),
                         np.asarray(L0_attn_w_out, np.float32),
                         np.asarray(L0_attn_b_out, np.float32),
                         np.asarray(L0_attn_w_gate, np.float32),
                         np.asarray(L0_attn_b_gate, np.float32),
                         np.asarray(L0_attn_pos_bias, np.float32),
                         np.asarray(L0_ffn_w1, np.float32),
                         np.asarray(L0_ffn_conv_w, np.float32),
                         np.asarray(L0_ffn_conv_b, np.float32),
                         np.asarray(L0_ffn_w2, np.float32))
    nc = build_program()
    in_maps = [stage_core(c, xF, consts) for c in range(8)]
    res = run_bass_kernel_spmd(nc, in_maps, core_ids=list(range(8)))
    out = np.concatenate([np.asarray(r["out_xT"]).T for r in res.results], axis=0)
    return np.ascontiguousarray(out[None]).astype(np.float32)


# version 11 (final)
# speedup vs baseline: 1.0862x; 1.0862x over previous
import numpy as np
import ml_dtypes
import concourse.bass as bass
import concourse.mybir as mybir
from concourse.bass_utils import run_bass_kernel_spmd
from concourse.bass_isa import ReduceOp
from concourse.library_config import attn as ATTN_LIB
from concourse.library_overlay import lower_extended_insts

F32 = mybir.dt.float32
BF16 = mybir.dt.bfloat16
I32 = mybir.dt.int32
AF = mybir.ActivationFunctionType
ALU = mybir.AluOpType
AX = mybir.AxisListType

OFFSETS = [(1 << j) * tau for j in range(11) for tau in range(4)]
DIST = sorted(set(OFFSETS))
NDIST = len(DIST)
GROUPS = [[i for i, o in enumerate(OFFSETS) if o == d] for d in DIST]
MAXG = max(len(g) for g in GROUPS)
H, HD, D, DH = 16, 64, 1024, 4096
NSEQ = 16384
U = 2048
HF = 32
HK = 3072 + HF
EXTN = HK + U + HF
TAN = U + 2 * HF
CH = 512
W7 = 7
WINC = W7 * CH

DEBUG_SCRATCH = False

PERM = np.zeros(D, np.int64)
for _g in range(8):
    for _p in range(128):
        PERM[128 * _g + _p] = 64 * (_p // 8) + 8 * _g + (_p % 8)


def _chunks(total, ch):
    out, a = [], 0
    while a < total:
        out.append((a, min(a + ch, total)))
        a = min(a + ch, total)
    return out


CHUNKS_E = _chunks(EXTN, CH)
TA0 = HK - HF
CHUNKS_TA = [(a, b) for (a, b) in CHUNKS_E if a >= TA0]
CHUNKS_P1 = _chunks(TA0, CH) + [(TA0 + 352 * i, TA0 + 352 * (i + 1)) for i in range(6)]
P3_RANGES = []
for _i in range(6):
    fa, fb = 352 * _i, 352 * (_i + 1)
    P3_RANGES.append((fa, fb, max(fa - 1, HF), min(fb - 1, HF + U)))


class KB:
    def __init__(self, nc):
        self.nc = nc
        self.ops = []
        self.sem = {}
        self.led = {}
        for name in ("dS", "dG", "dC", "tT", "tV", "tS", "tG"):
            self.sem[name] = nc.alloc_semaphore(name)
            self.led[name] = 0

    def on(self, eng, fn):
        self.ops.append((eng, fn))

    def dma(self, queue, out, in_):
        sname = {"sync": "dS", "gp": "dG", "sc": "dC"}[queue]
        sem = self.sem[sname]
        self.on(queue, lambda e, o=out, i=in_, s=sem: e.dma_start(out=o, in_=i).then_inc(s, 16))
        self.led[sname] += 16
        return self.led[sname]

    def wait(self, eng, sname, val):
        if val is None or val <= 0:
            return
        sem = self.sem[sname]
        self.on(eng, lambda e, s=sem, v=val: e.wait_ge(s, v))

    def step(self, eng, fn, inc=False):
        if inc:
            sname = {"te": "tT", "ve": "tV", "sc": "tS", "gp": "tG"}[eng]
            sem = self.sem[sname]
            self.on(eng, lambda e, f=fn, s=sem: f(e).then_inc(s, 1))
            self.led[sname] += 1
            return self.led[sname]
        self.on(eng, fn)
        return None

    def barrier_from(self, eng, *semnames):
        for s in semnames:
            self.wait(eng, s, self.led[s])

    def all_sync(self):
        for e in ("te", "ve", "sc", "sync", "gp"):
            for s in ("dS", "dG", "dC", "tT", "tV", "tS", "tG"):
                self.wait(e, s, self.led[s])


def build_program():
    nc = bass.Bass("TRN2", target_bir_lowering=False, debug=False,
                   detect_race_conditions=False)

    din = lambda n, shp, dt: nc.dram_tensor(n, list(shp), dt, kind="ExternalInput").ap()
    xT = din("xT", (D, EXTN), F32)
    wqkvg = din("wqkvg", (D, 4 * D), F32)
    bqkvg = din("bqkvg", (32, 128), F32)
    wout = din("wout", (D, D), F32)
    bout = din("bout", (8, 128), F32)
    w1T = din("w1T", (D, DH), F32)
    w2T = din("w2T", (DH, D), F32)
    cw = din("cw", (96, 128), F32)
    cb = din("cb", (32, 128), F32)
    pb = din("pb", (3, 128, MAXG), F32)
    maskE = din("maskE", (1, EXTN), BF16)
    mask2 = din("mask2", (1, TAN), F32)
    ident_b = din("ident_bf16", (128, 128), BF16)
    ident_f = din("ident_f32", (128, 128), F32)
    ssel_d = din("ssel", (8, 128, 128), BF16)
    asel_d = din("asel", (8, 128, 128), BF16)
    zsel_d = din("zsel", (128, 16), BF16)
    rsel_d = din("rsel", (16, 128), F32)
    onescol_b = din("ones_col_bf16", (128, 1), BF16)
    onescol_f = din("ones_col_f32", (128, 1), F32)
    onesrow_b = din("ones_row_bf16", (1, 128), BF16)
    onesrow_f = din("ones_row_f32", (1, 128), F32)
    n128col = din("n128_col", (128, 1), F32)
    p0125col = din("p0125_col", (128, 1), F32)
    p06col = din("p06_col", (128, 1), F32)

    out_xT = nc.dram_tensor("out_xT", [D, U], F32, kind="ExternalOutput").ap()
    skind = "ExternalOutput" if DEBUG_SCRATCH else "Internal"
    scrt = lambda n, shp, dt: nc.dram_tensor(n, list(shp), dt, kind=skind).ap()
    kD = scrt("kD", (D, EXTN), BF16)
    vD = scrt("vD", (D, EXTN), BF16)
    qD = scrt("qD", (D, TAN), BF16)
    gateD = scrt("gateD", (D, TAN), BF16)
    x1D = scrt("x1D", (D, TAN), F32)
    t2D = scrt("t2D", (DH, D), BF16)
    dbgD = scrt("dbgD", (128, 2816), BF16)
    dbg2D = scrt("dbg2D", (128, 11264), F32)

    AFN = 53208
    arena = nc.alloc_sbuf_tensor("arena", [128, AFN], F32)

    class Arena:
        def __init__(self):
            self.off = 0
            self.marks = []
            self.peak = 0

        def _adv(self, n):
            o = self.off
            self.off += n
            self.peak = max(self.peak, self.off)
            assert self.off <= AFN, f"arena overflow {self.off}"
            return o

        def f32(self, n):
            return arena[:, (o := self._adv(n)):o + n]

        def bf16(self, n):
            assert n % 2 == 0
            o = self._adv(n // 2)
            return arena[:, o:o + n // 2].bitcast(BF16)

        def i32(self, n):
            o = self._adv(n)
            return arena[:, o:o + n].bitcast(I32)

        def mark(self):
            self.marks.append(self.off)

        def release(self):
            self.off = self.marks.pop()

    A = Arena()
    c_ident_b = A.bf16(128)
    c_ident_f = A.f32(128)
    c_ssel = [A.bf16(128) for _ in range(8)]
    c_asel = [A.bf16(128) for _ in range(8)]
    c_zsel = A.bf16(16)
    c_rsel = A.f32(128)
    c_onescol_b = A.bf16(2)
    c_onescol_f = A.f32(1)
    c_onesrow_b = A.bf16(128)
    c_onesrow_f = A.f32(128)
    c_logB = [A.f32(1) for _ in range(3)]
    c_bcols = A.f32(32)
    c_bqs = A.f32(8)
    c_boutc = A.f32(8)
    c_cwc = A.f32(96)
    c_cbc = A.f32(32)
    c_n128 = A.f32(1)
    c_0125 = A.f32(1)
    c_off = A.f32(1)
    c_offt = A.i32(1)
    c_mc1 = A.f32(1)
    c_mc2 = A.f32(1)
    mw_acc = A.f32(44)

    PS = [nc.alloc_psum_tensor(f"ps{i}", [128, 512], F32) for i in range(8)]
    kb = KB(nc)

    def rsqrt_row(r_v, r_t0, r_t1, r_out, w, eps=1e-5, iters=4):
        kb.step("ve", lambda e: e.tensor_scalar(r_v[0:1, 0:w], r_v[0:1, 0:w], eps, None, ALU.add))
        kb.step("ve", lambda e: e.tensor_scalar(
            r_t0[0:1, 0:w].bitcast(I32), r_v[0:1, 0:w].bitcast(I32), 1, None, ALU.logical_shift_right))
        kb.step("ve", lambda e: e.tensor_scalar(
            r_t0[0:1, 0:w].bitcast(I32), r_t0[0:1, 0:w].bitcast(I32), 0x5F3759DF, -1,
            ALU.subtract, ALU.mult))
        for _ in range(iters):
            kb.step("ve", lambda e: e.tensor_mul(r_t1[0:1, 0:w], r_t0[0:1, 0:w], r_t0[0:1, 0:w]))
            kb.step("ve", lambda e: e.tensor_mul(r_t1[0:1, 0:w], r_t1[0:1, 0:w], r_v[0:1, 0:w]))
            kb.step("ve", lambda e: e.tensor_scalar(r_t1[0:1, 0:w], r_t1[0:1, 0:w], -0.5, 1.5,
                                                    ALU.mult, ALU.add))
            kb.step("ve", lambda e: e.tensor_mul(r_t0[0:1, 0:w], r_t0[0:1, 0:w], r_t1[0:1, 0:w]))
        kb.step("ve", lambda e: e.tensor_copy(r_out[0:1, 0:w], r_t0[0:1, 0:w]), inc=True)

    A.mark()
    wq_sb = A.bf16(8 * 4096)
    c_maskE = A.bf16(EXTN + 2)
    wstage = [A.f32(4096), A.f32(4096)]

    kb.on("gp", lambda e: e.load_library(ATTN_LIB))
    kb.dma("sync", c_maskE[0:1, 0:EXTN], maskE)
    maskE_val = kb.led["dS"]
    wcast_done = {}
    for k in range(8):
        if k >= 2:
            kb.wait("sc", "tS", wcast_done[k - 2])
        v = kb.dma("sc", wstage[k % 2][:, :], wqkvg[128 * k:128 * k + 128, :])
        kb.wait("sc", "dC", v)
        kb.step("sc", lambda e, k=k: e.activation(wq_sb[:, 4096 * k:4096 * (k + 1)],
                                                  wstage[k % 2][:, :], AF.Identity), inc=True)
        wcast_done[k] = kb.led["tS"]
    wcast_all = kb.led["tS"]
    for dst, src in ((c_ident_b, ident_b), (c_ident_f, ident_f),
                     (c_zsel, zsel_d), (c_rsel[0:16, :], rsel_d),
                     (c_onescol_b[:, 0:1], onescol_b), (c_onescol_f, onescol_f),
                     (c_onesrow_b[0:1, :], onesrow_b), (c_onesrow_f[0:1, :], onesrow_f),
                     (c_n128, n128col), (c_0125, p0125col), (c_off, p06col)):
        kb.dma("sync", dst, src)
    for j in range(8):
        kb.dma("sync", c_ssel[j], ssel_d[j])
        kb.dma("sync", c_asel[j], asel_d[j])

    A.mark()
    t_pb = A.f32(MAXG)
    t_pbe = A.f32(MAXG)
    t_sum = A.f32(1)
    t_rows = A.f32(128)
    for tau in range(3):
        kb.wait("sync", "tS", kb.led["tS"])
        v = kb.dma("sync", t_pb[:, 0:MAXG], pb[tau])
        kb.wait("sc", "dS", v)
        kb.step("sc", lambda e: e.activation(t_pbe[:, 0:MAXG], t_pb[:, 0:MAXG], AF.Exp), inc=True)
        kb.barrier_from("ve", "tS")
        kb.step("ve", lambda e: e.tensor_reduce(t_sum, t_pbe[:, 0:MAXG], AX.X, ALU.add), inc=True)
        kb.barrier_from("sc", "tV")
        kb.step("sc", lambda e, t=tau: e.activation(c_logB[t], t_sum, AF.Ln), inc=True)

    def col_load(dst_cols, src_ap, nrows, bank):
        kb.wait("sync", "tT", kb.led["tT"])
        v = kb.dma("sync", t_rows[0:nrows, 0:128], src_ap)
        kb.wait("te", "dS", v)
        kb.step("te", lambda e: e.matmul(bank[:, 0:nrows], t_rows[0:nrows, 0:128],
                                         c_ident_f[0:nrows, 0:nrows], start=True, stop=True), inc=True)
        kb.barrier_from("ve", "tT")
        kb.step("ve", lambda e: e.tensor_copy(dst_cols, bank[:, 0:nrows]), inc=True)
        kb.barrier_from("te", "tV")

    col_load(c_bcols[:, 0:32], bqkvg, 32, PS[0])
    col_load(c_boutc[:, 0:8], bout, 8, PS[1])
    col_load(c_cbc[:, 0:32], cb, 32, PS[2])
    col_load(c_cwc[:, 0:96], cw, 96, PS[3])
    kb.step("ve", lambda e: e.tensor_scalar(c_bqs[:, 0:8], c_bcols[:, 0:8], 0.125, None, ALU.mult),
            inc=True)
    kb.step("ve", lambda e: e.tensor_scalar(c_offt[:, :], c_off[:, :], 0.0, None, ALU.add),
            inc=True)
    kb.barrier_from("sc", "tV")
    kb.step("sc", lambda e: e.activation(c_off[:, :], c_offt[:, :], AF.Identity), inc=True)
    kb.barrier_from("ve", "tS")
    kb.step("ve", lambda e: e.tensor_scalar(c_off[:, :], c_off[:, :], -0.5, 128.5,
                                            ALU.mult, ALU.add), inc=True)
    A.release()
    p0_tV = kb.led["tV"]
    p0_tT = kb.led["tT"]
    p0_dS = kb.led["dS"]

    xch0 = A.f32(8 * CH)
    xchs = [xch0, wstage[0]]
    x16s = [A.bf16(8 * CH), wstage[1][:, 0:2048].bitcast(BF16)]
    xsqs = [A.bf16(8 * CH), wstage[1][:, 2048:4096].bitcast(BF16)]
    ln16s = [A.bf16(8 * CH), A.bf16(8 * CH)]
    msk_sb = [A.f32(CH), A.f32(CH)]
    lntmp = A.f32(CH)
    r_m = A.f32(CH)
    r_v = A.f32(CH)
    r_t0 = A.f32(CH)
    r_t1 = A.f32(CH)
    r_A = A.f32(CH)
    r_B = A.f32(CH)
    ev_bufs = [A.bf16(CH) for _ in range(8)]
    kv_bufs = [A.bf16(CH) for _ in range(8)]
    ps_sx, ps_sxx, ps_Ab, ps_Bb, ps_msk = PS[0], PS[1], PS[2], PS[3], PS[4]
    ps_mm = [PS[5], PS[6], PS[7]]

    fr = {}

    def front_FD(ci):
        a, b = CHUNKS_P1[ci]
        if ci == 0:
            kb.wait("sync", "tS", kb.led["tS"])
            kb.wait("sync", "tV", p0_tV)
            kb.wait("sync", "tT", p0_tT)
        if ci >= 2:
            kb.wait("sync", "tS", fr[ci - 2]["cast"])
            kb.wait("sync", "tV", fr[ci - 2]["ln"])
        if ci == 1:
            kb.wait("sync", "tS", wcast_all)
        v = kb.dma("sync", xchs[ci % 2][:, 0:8 * (b - a)].rearrange("p (g t) -> p g t", g=8),
                   xT.rearrange("(g r) t -> r g t", g=8)[:, :, a:b])
        fr.setdefault(ci, {})["dma"] = v

    def front_FC(ci):
        a, b = CHUNKS_P1[ci]
        w = b - a
        xch, x16, xsq = xchs[ci % 2], x16s[ci % 2], xsqs[ci % 2]
        kb.wait("sc", "dS", fr[ci]["dma"])
        if ci >= 2:
            kb.wait("sc", "tT", fr[ci - 2]["stats"])
        for g in range(8):
            kb.step("sc", lambda e, g=g, w=w, xch=xch, x16=x16: e.activation(
                x16[:, g * w:(g + 1) * w], xch[:, g * w:(g + 1) * w], AF.Identity))
            kb.step("sc", lambda e, g=g, w=w, xch=xch, xsq=xsq: e.activation(
                xsq[:, g * w:(g + 1) * w], xch[:, g * w:(g + 1) * w], AF.Square),
                    inc=(g == 7))
        fr[ci]["cast"] = kb.led["tS"]

    def front_FS(ci):
        a, b = CHUNKS_P1[ci]
        w = b - a
        x16, xsq = x16s[ci % 2], xsqs[ci % 2]
        kb.wait("te", "tS", fr[ci]["cast"])
        if ci == 0:
            kb.wait("te", "tV", p0_tV)
        for g in range(8):
            kb.step("te", lambda e, g=g, w=w, x16=x16: e.matmul(
                ps_sx[0:1, 0:w], c_onescol_b[:, 0:1], x16[:, g * w:(g + 1) * w],
                start=(g == 0), stop=(g == 7)))
        for g in range(8):
            kb.step("te", lambda e, g=g, w=w, xsq=xsq: e.matmul(
                ps_sxx[0:1, 0:w], c_onescol_b[:, 0:1], xsq[:, g * w:(g + 1) * w],
                start=(g == 0), stop=(g == 7)), inc=(g == 7))
        fr[ci]["stats"] = kb.led["tT"]

    def front_FR(ci):
        a, b = CHUNKS_P1[ci]
        w = b - a
        kb.wait("ve", "tT", fr[ci]["stats"])
        kb.step("ve", lambda e, w=w: e.tensor_scalar(r_m[0:1, 0:w], ps_sx[0:1, 0:w], 1.0 / D, None, ALU.mult))
        kb.step("ve", lambda e, w=w: e.tensor_scalar(r_v[0:1, 0:w], ps_sxx[0:1, 0:w], 1.0 / D, None, ALU.mult))
        kb.step("ve", lambda e, w=w: e.tensor_mul(r_t0[0:1, 0:w], r_m[0:1, 0:w], r_m[0:1, 0:w]))
        kb.step("ve", lambda e, w=w: e.tensor_sub(r_v[0:1, 0:w], r_v[0:1, 0:w], r_t0[0:1, 0:w]))
        rsqrt_row(r_v, r_t0, r_t1, r_A, w)
        kb.step("ve", lambda e, w=w: e.scalar_tensor_tensor(r_B[0:1, 0:w], r_m[0:1, 0:w], -1.0,
                                                            r_A[0:1, 0:w], ALU.mult, ALU.mult), inc=True)
        fr[ci]["rsqrt"] = kb.led["tV"]

    def front_FB(ci):
        a, b = CHUNKS_P1[ci]
        w = b - a
        kb.wait("te", "tV", fr[ci]["rsqrt"])
        if ci == 0:
            kb.wait("te", "dS", maskE_val)
        kb.step("te", lambda e, w=w: e.matmul(ps_Ab[:, 0:w], c_onesrow_f[0:1, :], r_A[0:1, 0:w],
                                              start=True, stop=True))
        kb.step("te", lambda e, w=w: e.matmul(ps_Bb[:, 0:w], c_onesrow_f[0:1, :], r_B[0:1, 0:w],
                                              start=True, stop=True))
        kb.step("te", lambda e, a=a, b=b, w=w: e.matmul(ps_msk[:, 0:w], c_onesrow_b[0:1, :],
                                                        c_maskE[0:1, a:b], start=True, stop=True),
                inc=True)
        fr[ci]["bcast"] = kb.led["tT"]

    def front_FL(ci):
        a, b = CHUNKS_P1[ci]
        w = b - a
        xch, ln16 = xchs[ci % 2], ln16s[ci % 2]
        kb.wait("ve", "tT", fr[ci]["bcast"])
        kb.step("ve", lambda e, w=w, ci=ci: e.tensor_copy(msk_sb[ci % 2][:, 0:w], ps_msk[:, 0:w]))
        for g in range(8):
            kb.step("ve", lambda e, g=g, w=w, xch=xch: e.tensor_mul(
                lntmp[:, 0:w], xch[:, g * w:(g + 1) * w], ps_Ab[:, 0:w]))
            kb.step("ve", lambda e, g=g, w=w, ln16=ln16: e.tensor_add(
                ln16[:, g * w:(g + 1) * w], lntmp[:, 0:w], ps_Bb[:, 0:w]), inc=(g == 7))
        fr[ci]["ln"] = kb.led["tV"]

    spill_led = []
    evac_done = []

    front_FD(0)
    front_FC(0)
    front_FS(0)
    front_FR(0)
    front_FB(0)
    front_FL(0)

    for ci, (a, b) in enumerate(CHUNKS_P1):
        w = b - a
        ln16 = ln16s[ci % 2]
        in_ta = a >= TA0
        mlist = list(range(32)) if in_ta else list(range(8, 24))
        L = len(mlist)
        ta_a = a - TA0
        nxt = ci + 1 if ci + 1 < len(CHUNKS_P1) else None
        if nxt is not None:
            front_FD(nxt)
        kb.wait("te", "tV", fr[ci]["ln"])
        if ci == 0:
            kb.wait("te", "tS", wcast_all)
        ve_pos = 10 if L == 16 else 23
        for mi, m in enumerate(mlist):
            blk, sub = m // 8, m % 8
            gmi = len(evac_done)
            bank = ps_mm[gmi % 3]
            ev = ev_bufs[gmi % 8]
            kv = kv_bufs[gmi % 8]
            if gmi >= 3:
                kb.wait("te", evac_done[gmi - 3][0], evac_done[gmi - 3][1])
            for k in range(8):
                kb.step("te", lambda e, k=k, m=m, w=w, bank=bank, ln16=ln16:
                        e.matmul(bank[:, 0:w], wq_sb[:, 4096 * k + 128 * m:4096 * k + 128 * m + 128],
                                 ln16[:, k * w:(k + 1) * w], start=(k == 0), stop=(k == 7)),
                        inc=(k == 7))
            kb.barrier_from("sc", "tT")
            if len(spill_led) >= 8:
                kb.wait("sc", "dG", spill_led[-8])
            if blk == 0:
                kb.step("sc", lambda e, w=w, sub=sub, bank=bank, ev=ev:
                        e.activation(ev[:, 0:w], bank[:, 0:w], AF.Identity,
                                     bias=c_bqs[:, sub:sub + 1], scale=c_0125), inc=True)
                evac_done.append(("tS", kb.led["tS"]))
                kb.barrier_from("gp", "tS")
                spill_led.append(kb.dma("gp", qD[128 * sub:128 * sub + 128, ta_a:ta_a + w], ev[:, 0:w]))
            elif blk == 3:
                kb.step("sc", lambda e, w=w, sub=sub, bank=bank, ev=ev:
                        e.activation(ev[:, 0:w], bank[:, 0:w], AF.Sigmoid,
                                     bias=c_bcols[:, 24 + sub:25 + sub]), inc=True)
                evac_done.append(("tS", kb.led["tS"]))
                kb.barrier_from("gp", "tS")
                spill_led.append(kb.dma("gp", gateD[128 * sub:128 * sub + 128, ta_a:ta_a + w], ev[:, 0:w]))
            else:
                kb.step("sc", lambda e, w=w, m=m, bank=bank, ev=ev:
                        e.activation(ev[:, 0:w], bank[:, 0:w], AF.Identity,
                                     bias=c_bcols[:, m:m + 1]), inc=True)
                kb.barrier_from("ve", "tS")
                if len(spill_led) >= 8:
                    kb.wait("ve", "dG", spill_led[-8])
                kb.step("ve", lambda e, w=w, ev=ev, kv=kv, ci=ci: e.tensor_mul(
                    kv[:, 0:w], ev[:, 0:w], msk_sb[ci % 2][:, 0:w]), inc=True)
                evac_done.append(("tV", kb.led["tV"]))
                kb.barrier_from("gp", "tV")
                dst = kD if blk == 1 else vD
                spill_led.append(kb.dma("gp", dst[128 * sub:128 * sub + 128, a:b], kv[:, 0:w]))
            if nxt is not None:
                if mi == 2:
                    front_FC(nxt)
                if mi == (8 if L == 16 else 21):
                    front_FS(nxt)
                if mi == ve_pos:
                    front_FR(nxt)
                if mi == (12 if L == 16 else 26):
                    front_FB(nxt)
                if mi == (13 if L == 16 else 27):
                    front_FL(nxt)
    A.release()
    kb.all_sync()

    A.mark()
    mw_st = [A.f32(512), A.f32(512)]
    mw_ab = A.bf16(512)
    kwin = A.bf16(8 * WINC)
    vwin = A.bf16(8 * WINC)
    wo_sb = A.bf16(8 * 1024)
    qch = A.bf16(8 * CH)
    gch = A.bf16(8 * CH)
    Esb = [A.bf16(CH) for _ in range(3)]
    zinv = A.f32(CH)
    zrep = A.f32(CH)
    atn = qch
    A.mark()
    arsb = [A.bf16(CH) for _ in range(NDIST)]
    pbuf = [A.bf16(CH) for _ in range(16)]
    wstage2 = A.f32(1024)
    A.release()
    A.mark()
    xc2 = A.f32(8 * CH)
    x1c = A.f32(8 * CH)
    A.release()
    A._adv(max(NDIST * CH // 2 + 8 * CH // 2 + 1024, 16 * CH))

    for g in range(8):
        kb.barrier_from("sc", "tS")
        v = kb.dma("sc", wstage2[:, :], wout[128 * g:128 * g + 128, :])
        kb.wait("sc", "dC", v)
        kb.step("sc", lambda e, g=g: e.activation(wo_sb[:, 1024 * g:1024 * (g + 1)],
                                                  wstage2[:, :], AF.Identity), inc=True)
    wocast_done = kb.led["tS"]

    def win_pieces(lo, hi):
        out, cur = [], lo
        while cur < hi:
            cend = min((cur // CH + 1) * CH, hi)
            out.append(((cur // CH) % W7 * CH + cur % CH, cend - cur, cur - lo))
            cur = cend
        return out

    mean_state = {"pc": 0, "marks": {}, "grp": 0}
    MEAN_PIECES = []
    for k in range(8):
        for j in range(8):
            MEAN_PIECES.append((w1T[128 * k:128 * k + 128, 512 * j:512 * (j + 1)],))
    for k in range(32):
        for j in range(2):
            MEAN_PIECES.append((w2T[128 * k:128 * k + 128, 512 * j:512 * (j + 1)],))

    def emit_mean_piece():
        st = mean_state
        pc = st["pc"]
        if pc >= len(MEAN_PIECES):
            return False
        (srcap,) = MEAN_PIECES[pc]
        slot = 8 + (pc % 32)
        if pc >= 2:
            kb.wait("sc", "tS", st["marks"][pc - 2])
        v = kb.dma("sc", mw_st[pc % 2][:, :], srcap)
        kb.wait("sc", "dC", v)
        kb.step("sc", lambda e, pc=pc, slot=slot: e.activation(
            mw_ab[:, :], mw_st[pc % 2][:, :], AF.Abs,
            accum_out=mw_acc[:, slot:slot + 1]), inc=True)
        st["marks"][pc] = kb.led["tS"]
        st["pc"] = pc + 1
        if (pc + 1) % 32 == 0:
            g = st["grp"]
            kb.barrier_from("ve", "tS")
            kb.step("ve", lambda e, g=g: e.tensor_reduce(mw_acc[:, g:g + 1], mw_acc[:, 8:40],
                                                         AX.X, ALU.add), inc=True)
            kb.barrier_from("sc", "tV")
            st["grp"] = g + 1
        return True

    spill_done = kb.led["dG"]
    kb.wait("sync", "dG", spill_done)
    for ci in range(6):
        a, b = CHUNKS_E[ci]
        slot = ci % W7
        for g in range(8):
            kb.dma("sync", kwin[:, g * WINC + slot * CH: g * WINC + slot * CH + (b - a)],
                   kD[128 * g:128 * g + 128, a:b])
            kb.dma("sync", vwin[:, g * WINC + slot * CH: g * WINC + slot * CH + (b - a)],
                   vD[128 * g:128 * g + 128, a:b])

    ps_S = [PS[0], PS[1], PS[2]]
    ps_Z, ps_ar, ps_av, ps_op = PS[3], PS[4], PS[5], PS[6]

    prev_end = {"tV": 0, "tS": 0}
    x1_spill = 0
    for ct, (a, b) in enumerate(CHUNKS_TA):
        w = b - a
        ta_a = a - TA0
        ci = 6 + ct
        ca_, cb_ = CHUNKS_E[ci]
        slot = ci % W7
        kb.wait("sync", "tV", prev_end["tV"])
        kb.wait("te", "tV", prev_end["tV"])
        for g in range(8):
            kb.dma("sync", kwin[:, g * WINC + slot * CH: g * WINC + slot * CH + (cb_ - ca_)],
                   kD[128 * g:128 * g + 128, ca_:cb_])
            kb.dma("sync", vwin[:, g * WINC + slot * CH: g * WINC + slot * CH + (cb_ - ca_)],
                   vD[128 * g:128 * g + 128, ca_:cb_])
        kb.dma("sync", qch[:, 0:8 * w].rearrange("p (g t) -> p g t", g=8),
               qD.rearrange("(g r) t -> r g t", g=8)[:, :, ta_a:ta_a + w])
        v = kb.dma("sync", gch[:, 0:8 * w].rearrange("p (g t) -> p g t", g=8),
                   gateD.rearrange("(g r) t -> r g t", g=8)[:, :, ta_a:ta_a + w])
        kb.wait("ve", "dS", v)
        kb.wait("te", "dS", v)
        if ct == 0:
            kb.wait("ve", "tS", wocast_done)
        else:
            kb.wait("ve", "dG", x1_spill)
            kb.wait("sc", "dG", x1_spill)

        te_di = {}
        for di in range(NDIST):
            tau, j, d = di // 8, di % 8, DIST[di]
            pieces = win_pieces(a - d, b - d)
            if di >= 2:
                kb.wait("ve", "tT", te_di[di - 2])
            sset = (di % 2) * 8
            for g in range(8):
                pb_ = pbuf[sset + g]
                for pi_, (wofs, ln_, ofs) in enumerate(pieces):
                    last = (g == 7 and pi_ == len(pieces) - 1)
                    kb.step("ve", lambda e, pb_=pb_, g=g, wofs=wofs, ln_=ln_, ofs=ofs, w=w:
                            e.tensor_mul(pb_[:, ofs:ofs + ln_],
                                         qch[:, g * w + ofs:g * w + ofs + ln_],
                                         kwin[:, g * WINC + wofs:g * WINC + wofs + ln_]), inc=last)
            kb.wait("te", "tV", kb.led["tV"])
            for g in range(8):
                st = (di % 8 == 0 and g == 0)
                sp = (di % 8 == 7 and g == 7)
                kb.step("te", lambda e, tau=tau, j=j, g=g, w=w, st=st, sp=sp, sset=sset:
                        e.matmul(ps_S[tau][:, 0:w], c_ssel[j], pbuf[sset + g][:, 0:w],
                                 start=st, stop=sp), inc=(g == 7))
            te_di[di] = kb.led["tT"]

        kb.barrier_from("sc", "tT")
        for tau in range(3):
            kb.step("sc", lambda e, tau=tau, w=w: e.activation(Esb[tau][:, 0:w], ps_S[tau][:, 0:w],
                                                               AF.Exp, bias=c_logB[tau]), inc=(tau == 2))
        kb.barrier_from("te", "tS")
        for tau in range(3):
            kb.step("te", lambda e, tau=tau, w=w: e.matmul(ps_Z[0:16, 0:w], c_zsel, Esb[tau][:, 0:w],
                                                           start=(tau == 0), stop=(tau == 2)),
                    inc=(tau == 2))
        kb.barrier_from("ve", "tT")
        kb.step("ve", lambda e, w=w: e.reciprocal(zinv[0:16, 0:w], ps_Z[0:16, 0:w]), inc=True)
        kb.barrier_from("te", "tV")
        kb.step("te", lambda e, w=w: e.matmul(ps_ar[:, 0:w], c_rsel[0:16, :], zinv[0:16, 0:w],
                                              start=True, stop=True), inc=True)
        kb.barrier_from("sc", "tT")
        kb.step("sc", lambda e, w=w: e.activation(zrep[:, 0:w], ps_ar[:, 0:w], AF.Identity), inc=True)
        kb.barrier_from("te", "tS")
        kb.barrier_from("sc", "tV")
        ar_sc = {}
        for di in range(NDIST):
            tau, j = di // 8, di % 8
            arbank = ps_ar if di % 2 == 0 else ps_av
            if di >= 2:
                kb.wait("te", "tS", ar_sc[di - 2])
            tt = kb.step("te", lambda e, tau=tau, j=j, w=w, arbank=arbank:
                         e.matmul(arbank[:, 0:w], c_asel[j], Esb[tau][:, 0:w], start=True, stop=True),
                         inc=True)
            kb.wait("sc", "tT", tt)
            ts = kb.step("sc", lambda e, di=di, w=w, arbank=arbank: e.activation(
                arsb[di][:, 0:w], arbank[:, 0:w], AF.Identity), inc=True)
            ar_sc[di] = ts
        kb.barrier_from("ve", "tS")
        av_evac = {}
        for g in range(8):
            te_mark = {}
            avbank = ps_av if g % 2 == 0 else ps_ar
            if g >= 2:
                kb.wait("te", "tV", av_evac[g - 2])
            for di in range(NDIST):
                d = DIST[di]
                pieces = win_pieces(a - d, b - d)
                pb_ = pbuf[8 + di % 4]
                if di >= 4:
                    kb.wait("ve", "tT", te_mark[di - 4])
                for pi_, (wofs, ln_, ofs) in enumerate(pieces):
                    last = (pi_ == len(pieces) - 1)
                    kb.step("ve", lambda e, pb_=pb_, g=g, di=di, wofs=wofs, ln_=ln_, ofs=ofs:
                            e.tensor_mul(pb_[:, ofs:ofs + ln_], arsb[di][:, ofs:ofs + ln_],
                                         vwin[:, g * WINC + wofs:g * WINC + wofs + ln_]), inc=last)
                kb.wait("te", "tV", kb.led["tV"])
                kb.step("te", lambda e, pb_=pb_, w=w, st=(di == 0), sp=(di == NDIST - 1), avbank=avbank:
                        e.matmul(avbank[:, 0:w], c_ident_b, pb_[:, 0:w], start=st, stop=sp), inc=True)
                te_mark[di] = kb.led["tT"]
            kb.barrier_from("ve", "tT")
            kb.step("ve", lambda e, g=g, w=w, avbank=avbank: e.tensor_mul(atn[:, g * w:(g + 1) * w],
                                                                          avbank[:, 0:w],
                                                                          zrep[:, 0:w]), inc=True)
            av_evac[g] = kb.led["tV"]
        for g in range(8):
            kb.step("ve", lambda e, g=g, w=w: e.tensor_mul(atn[:, g * w:(g + 1) * w],
                                                           atn[:, g * w:(g + 1) * w],
                                                           gch[:, g * w:(g + 1) * w]), inc=(g == 7))
        kb.barrier_from("te", "tV")
        kb.barrier_from("sync", "tV")
        v = kb.dma("sync", xc2[:, 0:8 * w].rearrange("p (g t) -> p g t", g=8),
                   xT.rearrange("(g r) t -> r g t", g=8)[:, :, a:b])
        kb.wait("ve", "dS", v)
        op_evac = {}
        for m in range(8):
            opbank = ps_op if m % 2 == 0 else ps_Z
            if m >= 2:
                kb.wait("te", "tV", op_evac[m - 2])
            for g in range(8):
                kb.step("te", lambda e, m=m, g=g, w=w, opbank=opbank:
                        e.matmul(opbank[:, 0:w], wo_sb[:, 1024 * g + 128 * m:1024 * g + 128 * m + 128],
                                 atn[:, g * w:(g + 1) * w], start=(g == 0), stop=(g == 7)),
                        inc=(g == 7))
            kb.barrier_from("ve", "tT")
            kb.step("ve", lambda e, m=m, w=w, opbank=opbank: e.scalar_tensor_tensor(
                x1c[:, m * w:(m + 1) * w], opbank[:, 0:w], c_boutc[:, m:m + 1],
                xc2[:, m * w:(m + 1) * w], ALU.add, ALU.add), inc=True)
            op_evac[m] = kb.led["tV"]
        kb.barrier_from("gp", "tV")
        for g in range(8):
            x1_spill = kb.dma("gp", x1D[128 * g:128 * g + 128, ta_a:ta_a + w],
                              x1c[:, g * w:(g + 1) * w])
        for _ in range(26):
            emit_mean_piece()
        prev_end = {"tV": kb.led["tV"], "tS": kb.led["tS"]}
    while emit_mean_piece():
        pass
    A.release()
    kb.all_sync()

    A.mark()
    tern1 = A.bf16(8 * 4096)
    A.mark()
    wst = [A.f32(4096) for _ in range(4)]
    qtmps = [A.i32(4096), A.i32(4096)]
    t2tiles = [A.bf16(1024), A.bf16(1024)]

    def finalize_mean(g0, g1, n, mc_dst, sw_slot):
        kb.step("ve", lambda e: e.tensor_add(mw_acc[:, 6:7], mw_acc[:, g0:g0 + 1],
                                             mw_acc[:, g1:g1 + 1]), inc=True)
        kb.barrier_from("te", "tV")
        kb.step("te", lambda e: e.matmul(PS[7][0:1, 0:1], c_onescol_f, mw_acc[:, 6:7],
                                         start=True, stop=True), inc=True)
        kb.barrier_from("ve", "tT")
        kb.step("ve", lambda e, n=n: e.tensor_scalar(mw_acc[0:1, 7:8], PS[7][0:1, 0:1],
                                                     1.0 / n, 1e-5, ALU.mult, ALU.max), inc=True)
        kb.barrier_from("te", "tV")
        kb.step("te", lambda e: e.matmul(PS[7][:, 0:1], c_onesrow_f[0:1, :], mw_acc[0:1, 7:8],
                                         start=True, stop=True), inc=True)
        kb.barrier_from("ve", "tT")
        kb.step("ve", lambda e: e.tensor_copy(mc_dst, PS[7][:, 0:1]))
        kb.step("ve", lambda e, s=sw_slot: e.reciprocal(mw_acc[:, s:s + 1], PS[7][:, 0:1]), inc=True)
        kb.barrier_from("ve", "tV")

    def quant_pass(wT, ntiles, fdim, tern_dst, t2_spill, sw_slot):
        ve_mark = {}
        sc_mark = {}
        dma_led = {}
        kb.barrier_from("sync", "tV")
        for k in range(min(4, ntiles)):
            dma_led[k] = kb.dma("sync", wst[k % 4][:, 0:fdim], wT[128 * k:128 * k + 128, :])
        for k in range(ntiles):
            kb.wait("ve", "dS", dma_led[k])
            if k >= 2:
                kb.wait("ve", "tS", sc_mark[k - 2])
            qt = qtmps[k % 2]
            kb.step("ve", lambda e, fdim=fdim, k=k, qt=qt, s=sw_slot: e.tensor_scalar(
                qt[:, 0:fdim], wst[k % 4][:, 0:fdim], mw_acc[:, s:s + 1], c_off[:, 0:1],
                ALU.mult, ALU.add))
            kb.step("ve", lambda e, fdim=fdim, qt=qt: e.tensor_scalar(
                qt[:, 0:fdim], qt[:, 0:fdim], 129, 127, ALU.min, ALU.max), inc=True)
            ve_mark[k] = kb.led["tV"]
            kb.barrier_from("sc", "tV")
            if tern_dst is not None:
                kb.step("sc", lambda e, k=k, fdim=fdim, qt=qt: e.activation(
                    tern_dst[:, fdim * k:fdim * (k + 1)], qt[:, 0:fdim], AF.Identity,
                    bias=c_n128), inc=True)
            else:
                t2t = t2tiles[k % 2]
                if k >= 2:
                    kb.wait("sc", "dG", spill_led2[k - 2])
                kb.step("sc", lambda e, fdim=fdim, t2t=t2t, qt=qt: e.activation(
                    t2t[:, 0:fdim], qt[:, 0:fdim], AF.Identity, bias=c_n128), inc=True)
                kb.barrier_from("gp", "tS")
                spill_led2.append(kb.dma("gp", t2_spill[128 * k:128 * k + 128, :], t2t[:, 0:fdim]))
            sc_mark[k] = kb.led["tS"]
            if k + 4 < ntiles:
                kb.wait("sync", "tV", ve_mark[k])
                dma_led[k + 4] = kb.dma("sync", wst[k % 4][:, 0:fdim],
                                        wT[128 * (k + 4):128 * (k + 4) + 128, :])

    spill_led2 = []
    finalize_mean(0, 1, 8 * 512 * 128 * 8, c_mc1, 40)
    finalize_mean(2, 3, 64 * 512 * 128, c_mc2, 41)
    quant_pass(w1T, 8, 4096, tern1, None, 40)
    quant_pass(w2T, 32, 1024, None, t2D, 41)
    A.release()
    kb.all_sync()

    A.mark()
    SILW = 352
    XS = SILW + 4
    m2bs = [A.f32(SILW), A.f32(SILW)]
    x1ps = [A.f32(8 * XS), A.f32(8 * XS)]
    ln2t = A.f32(SILW)
    ln2q = ln2t
    xq1s = [A.bf16(8 * SILW), A.bf16(8 * SILW)]
    Mt = A.f32(SILW)
    srepT = A.f32(SILW)
    rr1Ts = [A.f32(SILW), A.f32(SILW)]
    AsT = A.f32(SILW)
    BsT = A.f32(SILW)
    s2T = A.f32(SILW)
    rr2T = A.f32(SILW)
    U1 = A.f32(SILW)
    U0 = A.f32(SILW)
    silu = A.f32(32 * SILW)
    xq2 = A.bf16(4 * SILW)
    sqb8 = A.bf16(8 * SILW)
    hbufs = [A.f32(XS), A.f32(XS)]
    ybufs = [A.f32(SILW), A.f32(SILW)]
    htail = A.f32(64)
    qi32s = [A.i32(SILW), A.i32(SILW)]
    t2st = [A.bf16(1024), A.bf16(1024)]
    x2bufs = [A.f32(SILW) for _ in range(2)]
    asb = [A.f32(SILW), A.f32(SILW)]
    r_m3 = A.f32(SILW)
    r_v3 = A.f32(SILW)
    r_t03 = A.f32(SILW)
    r_t13 = A.f32(SILW)
    r_A3 = A.f32(SILW)
    r_B3 = A.f32(SILW)

    def geo(i):
        fa, fb, ca3, cb3 = P3_RANGES[i]
        x1_lo = min(fa, ca3)
        return fa, fb, ca3, cb3, fb - fa, cb3 - ca3, x1_lo, fa - x1_lo

    f3 = {}
    ybm_l, hd_l, sil_l, abs_l, max_l = {}, {}, {}, {}, {}
    teK_l, scK_l, qf_sc, vq_l = {}, {}, {}, {}
    oute, ffn1_end = {}, {}
    out_spills = []

    def f3_D(i):
        fa, fb, ca3, cb3, w3, wc, x1_lo, xo = geo(i)
        if i >= 2:
            kb.wait("sync", "tV", oute[i - 2])
            kb.wait("sync", "tS", f3[i - 2]["cast"])
            kb.wait("sync", "tT", f3[i - 2]["bcast"])
        d = f3.setdefault(i, {})
        d["mdma"] = kb.dma("sync", m2bs[i % 2][0:1, 0:w3], mask2[0:1, fa:fa + w3])
        d["xdma"] = kb.dma("sync",
                           x1ps[i % 2].rearrange("p (g t) -> p g t", t=XS)[:, :, 0:fb - x1_lo],
                           x1D.rearrange("(g r) t -> r g t", g=8)[:, :, x1_lo:fb])

    def f3_C(i):
        fa, fb, ca3, cb3, w3, wc, x1_lo, xo = geo(i)
        x1p, xb = x1ps[i % 2], xq1s[i % 2]
        kb.wait("sc", "dS", f3[i]["xdma"])
        if i >= 1:
            kb.wait("sc", "tT", f3[i - 1]["stats"])
        if i >= 2:
            kb.wait("sc", "tT", ffn1_end[i - 2])
        for g in range(8):
            kb.step("sc", lambda e, g=g, w3=w3, xo=xo, x1p=x1p, xb=xb: e.activation(
                xb[:, g * w3:(g + 1) * w3], x1p[:, g * XS + xo:g * XS + xo + w3], AF.Identity))
            kb.step("sc", lambda e, g=g, w3=w3, xo=xo, x1p=x1p: e.activation(
                sqb8[:, g * w3:(g + 1) * w3], x1p[:, g * XS + xo:g * XS + xo + w3], AF.Square),
                    inc=(g == 7))
        f3[i]["cast"] = kb.led["tS"]

    def f3_U(i):
        fa, fb, ca3, cb3, w3, wc, x1_lo, xo = geo(i)
        x1p = x1ps[i % 2]
        kb.wait("ve", "dS", f3[i]["xdma"])
        for g in range(8):
            xg = lambda g=g, x1p=x1p, xo=xo, w3=w3: x1p[:, g * XS + xo:g * XS + xo + w3]
            if g == 0:
                kb.step("ve", lambda e, xg=xg, w3=w3: e.tensor_copy(U1[:, 0:w3], xg()))
                kb.step("ve", lambda e, xg=xg, w3=w3: e.tensor_scalar(
                    U0[:, 0:w3], xg(), -1.0, None, ALU.mult))
            else:
                kb.step("ve", lambda e, xg=xg, w3=w3: e.tensor_max(U1[:, 0:w3], U1[:, 0:w3], xg()))
                kb.step("ve", lambda e, xg=xg, w3=w3: e.scalar_tensor_tensor(
                    U0[:, 0:w3], xg(), -1.0, U0[:, 0:w3], ALU.mult, ALU.max), inc=(g == 7))
        mm = kb.led["tV"]
        kb.wait("gp", "tV", mm)
        kb.step("gp", lambda e, w3=w3: e.partition_all_reduce(
            U1[:, 0:w3], U1[:, 0:w3], 128, ReduceOp.max), inc=True)
        kb.step("gp", lambda e, w3=w3: e.partition_all_reduce(
            U0[:, 0:w3], U0[:, 0:w3], 128, ReduceOp.max), inc=True)
        f3[i]["gpu"] = kb.led["tG"]

    def f3_S(i):
        fa, fb, ca3, cb3, w3, wc, x1_lo, xo = geo(i)
        xb = xq1s[i % 2]
        kb.wait("te", "tS", f3[i]["cast"])
        for g in range(8):
            kb.step("te", lambda e, g=g, w3=w3, xb=xb: e.matmul(
                PS[0][0:1, 0:w3], c_onescol_b[:, 0:1], xb[:, g * w3:(g + 1) * w3],
                start=(g == 0), stop=(g == 7)))
        for g in range(8):
            kb.step("te", lambda e, g=g, w3=w3: e.matmul(
                PS[1][0:1, 0:w3], c_onescol_b[:, 0:1], sqb8[:, g * w3:(g + 1) * w3],
                start=(g == 0), stop=(g == 7)), inc=(g == 7))
        f3[i]["stats"] = kb.led["tT"]

    def f3_R(i):
        fa, fb, ca3, cb3, w3, wc, x1_lo, xo = geo(i)
        kb.wait("ve", "tT", f3[i]["stats"])
        kb.step("ve", lambda e, w3=w3: e.tensor_scalar(r_m3[0:1, 0:w3], PS[0][0:1, 0:w3], 1.0 / D,
                                                       None, ALU.mult))
        kb.step("ve", lambda e, w3=w3: e.tensor_scalar(r_v3[0:1, 0:w3], PS[1][0:1, 0:w3], 1.0 / D,
                                                       None, ALU.mult))
        kb.step("ve", lambda e, w3=w3: e.tensor_mul(r_t03[0:1, 0:w3], r_m3[0:1, 0:w3], r_m3[0:1, 0:w3]))
        kb.step("ve", lambda e, w3=w3: e.tensor_sub(r_v3[0:1, 0:w3], r_v3[0:1, 0:w3], r_t03[0:1, 0:w3]))
        rsqrt_row(r_v3, r_t03, r_t13, r_A3, w3)
        kb.step("ve", lambda e, w3=w3: e.scalar_tensor_tensor(r_B3[0:1, 0:w3], r_m3[0:1, 0:w3], -1.0,
                                                              r_A3[0:1, 0:w3], ALU.mult, ALU.mult),
                inc=True)
        f3[i]["rsqrt"] = kb.led["tV"]

    def f3_B(i):
        fa, fb, ca3, cb3, w3, wc, x1_lo, xo = geo(i)
        kb.wait("te", "tV", f3[i]["rsqrt"])
        kb.wait("te", "dS", f3[i]["mdma"])
        kb.step("te", lambda e, w3=w3: e.matmul(PS[2][:, 0:w3], c_onesrow_f[0:1, :], r_A3[0:1, 0:w3],
                                                start=True, stop=True))
        kb.step("te", lambda e, w3=w3: e.matmul(PS[3][:, 0:w3], c_onesrow_f[0:1, :], r_B3[0:1, 0:w3],
                                                start=True, stop=True))
        kb.step("te", lambda e, w3=w3, i=i: e.matmul(PS[0][:, 0:w3], c_onesrow_f[0:1, :],
                                                     m2bs[i % 2][0:1, 0:w3], start=True, stop=True),
                inc=True)
        f3[i]["bcast"] = kb.led["tT"]

    def f3_T(i):
        fa, fb, ca3, cb3, w3, wc, x1_lo, xo = geo(i)
        kb.wait("ve", "tT", f3[i]["bcast"])
        kb.wait("ve", "tG", f3[i]["gpu"])
        kb.step("ve", lambda e, w3=w3: e.tensor_mul(ln2t[:, 0:w3], U1[:, 0:w3], PS[2][:, 0:w3]))
        kb.step("ve", lambda e, w3=w3: e.tensor_add(U1[:, 0:w3], ln2t[:, 0:w3], PS[3][:, 0:w3]))
        kb.step("ve", lambda e, w3=w3: e.tensor_mul(ln2t[:, 0:w3], U0[:, 0:w3], PS[2][:, 0:w3]))
        kb.step("ve", lambda e, w3=w3: e.tensor_sub(U0[:, 0:w3], ln2t[:, 0:w3], PS[3][:, 0:w3]))
        kb.step("ve", lambda e, w3=w3: e.tensor_max(U1[:, 0:w3], U1[:, 0:w3], U0[:, 0:w3]))
        kb.step("ve", lambda e, w3=w3: e.tensor_scalar(U1[:, 0:w3], U1[:, 0:w3], 1e-5, None, ALU.max))
        kb.step("ve", lambda e, w3=w3: e.reciprocal(srepT[:, 0:w3], U1[:, 0:w3]))
        kb.step("ve", lambda e, w3=w3: e.tensor_scalar(srepT[:, 0:w3], srepT[:, 0:w3], 127.0,
                                                       None, ALU.mult))
        kb.step("ve", lambda e, w3=w3: e.tensor_mul(srepT[:, 0:w3], srepT[:, 0:w3], PS[0][:, 0:w3]))
        kb.step("ve", lambda e, w3=w3, i=i: e.tensor_scalar(rr1Ts[i % 2][:, 0:w3], U1[:, 0:w3],
                                                            c_mc1[:, 0:1], 1.0 / 127.0,
                                                            ALU.mult, ALU.mult))
        kb.step("ve", lambda e, w3=w3: e.tensor_mul(AsT[:, 0:w3], srepT[:, 0:w3], PS[2][:, 0:w3]))
        kb.step("ve", lambda e, w3=w3: e.tensor_mul(BsT[:, 0:w3], srepT[:, 0:w3], PS[3][:, 0:w3]))
        kb.step("ve", lambda e, w3=w3: e.tensor_scalar(BsT[:, 0:w3], BsT[:, 0:w3], c_off[:, 0:1],
                                                       None, ALU.add), inc=True)
        f3[i]["scl"] = kb.led["tV"]

    def f3_Q(i):
        fa, fb, ca3, cb3, w3, wc, x1_lo, xo = geo(i)
        x1p, xq1 = x1ps[i % 2], xq1s[i % 2]
        for g in range(8):
            gq = i * 8 + g
            qi = qi32s[g % 2]
            if g < 2 and i >= 2:
                kb.wait("ve", "tS", scK_l[(i - 1) * 32 - 1])
            if gq >= 2:
                kb.wait("ve", "tS", qf_sc[gq - 2])
            kb.step("ve", lambda e, g=g, w3=w3, xo=xo, x1p=x1p: e.tensor_mul(
                ln2t[:, 0:w3], x1p[:, g * XS + xo:g * XS + xo + w3], AsT[:, 0:w3]))
            kb.step("ve", lambda e, w3=w3, qi=qi: e.tensor_add(qi[:, 0:w3], ln2t[:, 0:w3],
                                                               BsT[:, 0:w3]), inc=True)
            kb.barrier_from("sc", "tV")
            if g == 0:
                kb.wait("sc", "tT", f3[i]["stats"])
            kb.step("sc", lambda e, g=g, w3=w3, qi=qi, xq1=xq1: e.activation(
                xq1[:, g * w3:(g + 1) * w3], qi[:, 0:w3], AF.Identity, bias=c_n128), inc=True)
            qf_sc[gq] = kb.led["tS"]
        f3[i]["xq1"] = kb.led["tS"]

    f3_D(0)
    f3_C(0)
    f3_U(0)
    f3_S(0)
    f3_R(0)
    f3_B(0)
    f3_T(0)
    f3_Q(0)

    for i3 in range(len(P3_RANGES)):
        fa, fb, ca3, cb3, w3, wc, x1_lo, xo = geo(i3)
        nxt = i3 + 1 if i3 + 1 < len(P3_RANGES) else None
        if nxt is not None:
            f3_D(nxt)
        xq1 = xq1s[i3 % 2]
        rr1T = rr1Ts[i3 % 2]
        x1p = x1ps[i3 % 2]
        kb.wait("te", "tS", f3[i3]["xq1"])
        if i3 >= 1:
            kb.wait("te", "tV", oute[i3 - 1])
        lo = ca3 - 1 - fa + 2
        for m in range(32):
            gm = i3 * 32 + m
            bank = PS[4 + m % 4]
            yb = ybufs[m % 2]
            hb = hbufs[m % 2]
            if m >= 4:
                kb.wait("te", "tV", hd_l[gm - 4])
            for k in range(8):
                kb.step("te", lambda e, k=k, m=m, w3=w3, bank=bank, xq1=xq1:
                        e.matmul(bank[:, 0:w3],
                                 tern1[:, 4096 * k + 128 * m:4096 * k + 128 * m + 128],
                                 xq1[:, k * w3:(k + 1) * w3], start=(k == 0), stop=(k == 7)),
                        inc=(k == 7))
            kb.barrier_from("ve", "tT")
            if i3 > 0:
                kb.step("ve", lambda e, m=m, hb=hb: e.tensor_copy(hb[:, 0:2],
                                                                  htail[:, 2 * m:2 * m + 2]))
            if gm >= 2:
                kb.wait("ve", "tS", ybm_l[gm - 2])
            kb.step("ve", lambda e, m=m, w3=w3, bank=bank, hb=hb, rr1T=rr1T: e.tensor_mul(
                hb[:, 2:2 + w3], bank[:, 0:w3], rr1T[:, 0:w3]), inc=True)
            hd_l[gm] = kb.led["tV"]
            kb.barrier_from("sc", "tV")
            if i3 >= 1:
                kb.wait("sc", "tV", vq_l[gm - 32])
            kb.step("sc", lambda e, m=m, lo=lo, wc=wc, yb=yb, hb=hb: e.activation(
                yb[:, 0:wc], hb[:, lo:lo + wc], AF.Identity,
                bias=c_cbc[:, m:m + 1], scale=c_cwc[:, m:m + 1]), inc=True)
            ybm_l[gm] = kb.led["tS"]
            if m >= 1:
                kb.wait("sc", "tS", sil_l[gm - 1])
                if gm - 1 >= 2:
                    kb.wait("sc", "tV", max_l.get(gm - 3, 0) or 0)
                kb.step("sc", lambda e, m=m, wc=wc: e.activation(
                    asb[(m - 1) % 2][:, 0:wc], silu[:, (m - 1) * SILW:(m - 1) * SILW + wc],
                    AF.Abs), inc=True)
                abs_l[gm - 1] = kb.led["tS"]
            if m >= 1:
                kb.wait("ve", "tS", abs_l[gm - 1])
                if m == 1:
                    kb.step("ve", lambda e, wc=wc: e.tensor_copy(Mt[:, 0:wc], asb[0][:, 0:wc]),
                            inc=True)
                else:
                    kb.step("ve", lambda e, m=m, wc=wc: e.tensor_max(
                        Mt[:, 0:wc], Mt[:, 0:wc], asb[(m - 1) % 2][:, 0:wc]), inc=True)
                max_l[gm - 1] = kb.led["tV"]
            kb.wait("ve", "tS", ybm_l[gm])
            kb.step("ve", lambda e, m=m, lo=lo, wc=wc, yb=yb, hb=hb: e.scalar_tensor_tensor(
                yb[:, 0:wc], hb[:, lo + 1:lo + 1 + wc], c_cwc[:, 32 + m:33 + m], yb[:, 0:wc],
                ALU.mult, ALU.add))
            kb.step("ve", lambda e, m=m, lo=lo, wc=wc, yb=yb, hb=hb: e.scalar_tensor_tensor(
                yb[:, 0:wc], hb[:, lo + 2:lo + 2 + wc], c_cwc[:, 64 + m:65 + m], yb[:, 0:wc],
                ALU.mult, ALU.add), inc=True)
            kb.step("ve", lambda e, m=m, w3=w3, hb=hb: e.tensor_copy(htail[:, 2 * m:2 * m + 2],
                                                                     hb[:, w3:w3 + 2]))
            kb.barrier_from("sc", "tV")
            kb.step("sc", lambda e, m=m, wc=wc, yb=yb: e.activation(
                silu[:, m * SILW:m * SILW + wc], yb[:, 0:wc], AF.Silu), inc=True)
            sil_l[gm] = kb.led["tS"]
            if nxt is not None:
                if m == 2:
                    f3_C(nxt)
                if m == 4:
                    f3_U(nxt)
                if m == 8:
                    f3_S(nxt)
                if m == 10:
                    f3_R(nxt)
                if m == 16:
                    f3_B(nxt)
                if m == 18:
                    f3_T(nxt)
                if m == 20:
                    f3_Q(nxt)
        ffn1_end[i3] = kb.led["tT"]
        gm31 = i3 * 32 + 31
        kb.wait("sc", "tS", sil_l[gm31])
        kb.step("sc", lambda e, wc=wc: e.activation(asb[31 % 2][:, 0:wc],
                                                    silu[:, 31 * SILW:31 * SILW + wc], AF.Abs),
                inc=True)
        abs_l[gm31] = kb.led["tS"]
        kb.wait("ve", "tS", abs_l[gm31])
        kb.step("ve", lambda e, wc=wc: e.tensor_max(Mt[:, 0:wc], Mt[:, 0:wc],
                                                    asb[31 % 2][:, 0:wc]), inc=True)
        max_l[gm31] = kb.led["tV"]
        kb.barrier_from("gp", "tV")
        kb.step("gp", lambda e, wc=wc: e.partition_all_reduce(
            Mt[:, 0:wc], Mt[:, 0:wc], 128, ReduceOp.absmax), inc=True)
        kb.wait("ve", "tG", kb.led["tG"])
        kb.step("ve", lambda e, wc=wc: e.tensor_scalar(Mt[:, 0:wc], Mt[:, 0:wc], 1e-5, None, ALU.max))
        kb.step("ve", lambda e, wc=wc: e.reciprocal(s2T[:, 0:wc], Mt[:, 0:wc]))
        kb.step("ve", lambda e, wc=wc: e.tensor_scalar(s2T[:, 0:wc], s2T[:, 0:wc], 127.0,
                                                       None, ALU.mult))
        kb.step("ve", lambda e, wc=wc: e.tensor_scalar(rr2T[:, 0:wc], Mt[:, 0:wc], c_mc2[:, 0:1],
                                                       1.0 / 127.0, ALU.mult, ALU.mult), inc=True)
        if DEBUG_SCRATCH and i3 == 3:
            kb.wait("gp", "tS", sil_l[gm31])
            kb.dma("gp", dbgD, xq1)
            kb.dma("gp", dbg2D, silu)
        for k in range(32):
            gk = i3 * 32 + k
            qi = qi32s[k % 2]
            if k < 2 and nxt is not None:
                kb.wait("ve", "tS", f3[nxt]["xq1"])
            if gk >= 2:
                kb.wait("ve", "tS", scK_l[gk - 2])
            kb.step("ve", lambda e, k=k, wc=wc: e.tensor_mul(ln2q[:, 0:wc],
                                                             silu[:, k * SILW:k * SILW + wc],
                                                             s2T[:, 0:wc]))
            kb.step("ve", lambda e, wc=wc, qi=qi: e.tensor_scalar(qi[:, 0:wc], ln2q[:, 0:wc],
                                                                  c_off[:, 0:1], None, ALU.add),
                    inc=True)
            vq_l[gk] = kb.led["tV"]
            kb.barrier_from("sc", "tV")
            if gk >= 4:
                kb.wait("sc", "tT", teK_l[gk - 4])
            kb.step("sc", lambda e, k=k, wc=wc, qi=qi: e.activation(
                xq2[:, (k % 4) * SILW:(k % 4) * SILW + wc], qi[:, 0:wc], AF.Identity,
                bias=c_n128), inc=True)
            scK_l[gk] = kb.led["tS"]
            if gk >= 2:
                kb.wait("sync", "tT", teK_l[gk - 2])
            v = kb.dma("sync", t2st[k % 2][:, :], t2D[128 * k:128 * k + 128, :])
            kb.wait("te", "dS", v)
            kb.wait("te", "tS", scK_l[gk])
            if k == 0:
                kb.wait("te", "tV", hd_l[gm31])
                if nxt is not None:
                    kb.wait("te", "tV", f3[nxt]["scl"])
            for m2 in range(8):
                kb.step("te", lambda e, k=k, m2=m2, wc=wc:
                        e.matmul(PS[m2][:, 0:wc], t2st[k % 2][:, 128 * m2:128 * m2 + 128],
                                 xq2[:, (k % 4) * SILW:(k % 4) * SILW + wc],
                                 start=(k == 0), stop=(k == 31)), inc=(m2 == 7))
            teK_l[gk] = kb.led["tT"]
        kb.barrier_from("ve", "tT")
        for m2 in range(8):
            x2b = x2bufs[m2 % 2]
            if len(out_spills) >= 2:
                kb.wait("ve", "dG", out_spills[-2])
            kb.step("ve", lambda e, m2=m2, wc=wc, x2b=x2b: e.tensor_mul(x2b[:, 0:wc], PS[m2][:, 0:wc],
                                                                        rr2T[:, 0:wc]))
            kb.step("ve", lambda e, m2=m2, wc=wc, co=ca3 - x1_lo, x2b=x2b, x1p=x1p: e.tensor_add(
                x2b[:, 0:wc], x2b[:, 0:wc],
                x1p[:, m2 * XS + co:m2 * XS + co + wc]), inc=True)
            kb.barrier_from("gp", "tV")
            out_spills.append(kb.dma("gp", out_xT[128 * m2:128 * m2 + 128, ca3 - HF:cb3 - HF],
                                     x2b[:, 0:wc]))
        oute[i3] = kb.led["tV"]
    A.release()
    kb.all_sync()

    import contextlib
    with contextlib.ExitStack() as es:
        block = es.enter_context(nc.Block())

        def mk(engname):
            def body(e):
                for eng, fn in kb.ops:
                    if eng == engname:
                        fn(e)
            return body

        block.sync(mk("sync"))
        block.gpsimd(mk("gp"))
        block.tensor(mk("te"))
        block.vector(mk("ve"))
        block.scalar(mk("sc"))
    lower_extended_insts(nc)
    return nc


def make_consts(w_qkv, b_qkv, w_out, b_out, w_gate, b_gate, pos_bias, w1,
                conv_w, conv_b, w2):
    consts = {}
    consts["wqkvg"] = np.ascontiguousarray(np.concatenate(
        [w_qkv[:, 0:D][:, PERM], w_qkv[:, D:2 * D][:, PERM],
         w_qkv[:, 2 * D:][:, PERM], w_gate[:, PERM]], axis=1))
    consts["bqkvg"] = np.ascontiguousarray(np.concatenate(
        [b_qkv[0:D][PERM], b_qkv[D:2 * D][PERM], b_qkv[2 * D:][PERM],
         b_gate[PERM]]).reshape(32, 128))
    consts["wout"] = np.ascontiguousarray(w_out[PERM, :])
    consts["bout"] = np.ascontiguousarray(b_out.reshape(8, 128))
    consts["w1T"] = np.ascontiguousarray(w1.T)
    consts["w2T"] = np.ascontiguousarray(w2.T)
    consts["cw"] = np.ascontiguousarray(conv_w[:, 0, :].T.reshape(3, 32, 128).reshape(96, 128))
    consts["cb"] = np.ascontiguousarray(conv_b.reshape(32, 128))
    pbb = np.full((3, 128, MAXG), -1e30, np.float32)
    for di in range(NDIST):
        tau, j = di // 8, di % 8
        for i, src in enumerate(GROUPS[di]):
            for h in range(H):
                pbb[tau, 16 * j + h, i] = pos_bias[src, h]
    consts["pb"] = pbb
    consts["ident_bf16"] = np.eye(128, dtype=ml_dtypes.bfloat16)
    consts["ident_f32"] = np.eye(128, dtype=np.float32)
    ssel = np.zeros((8, 128, 128), np.float32)
    asel = np.zeros((8, 128, 128), np.float32)
    for j in range(8):
        for p in range(128):
            ssel[j, p, 16 * j + p // 8] = 1.0
            asel[j, 16 * j + p // 8, p] = 1.0
    consts["ssel"] = ssel.astype(ml_dtypes.bfloat16)
    consts["asel"] = asel.astype(ml_dtypes.bfloat16)
    zsel = np.zeros((128, 16), np.float32)
    for p in range(128):
        zsel[p, p % 16] = 1.0
    consts["zsel"] = zsel.astype(ml_dtypes.bfloat16)
    rsel = np.zeros((16, 128), np.float32)
    for p in range(128):
        rsel[p // 8, p] = 1.0
    consts["rsel"] = rsel
    consts["ones_col_bf16"] = np.ones((128, 1), ml_dtypes.bfloat16)
    consts["ones_col_f32"] = np.ones((128, 1), np.float32)
    consts["ones_row_bf16"] = np.ones((1, 128), ml_dtypes.bfloat16)
    consts["ones_row_f32"] = np.ones((1, 128), np.float32)
    consts["n128_col"] = np.full((128, 1), -128.0, np.float32)
    consts["p0125_col"] = np.full((128, 1), 0.125, np.float32)
    consts["p06_col"] = np.full((128, 1), 0.6, np.float32)
    return consts


def stage_core(c, xF, consts):
    t0 = c * U
    ext_lo = t0 - HK
    xs = np.zeros((D, EXTN), np.float32)
    lo, hi = max(ext_lo, 0), min(ext_lo + EXTN, NSEQ)
    xs[:, lo - ext_lo: hi - ext_lo] = xF[:, lo:hi]
    idx = np.arange(ext_lo, ext_lo + EXTN)
    mE = ((idx >= 0) & (idx < NSEQ)).astype(ml_dtypes.bfloat16)[None, :]
    idx2 = np.arange(t0 - HF, t0 + U + HF)
    m2 = ((idx2 >= 0) & (idx2 < NSEQ)).astype(np.float32)[None, :]
    d = dict(consts)
    d["xT"] = xs
    d["maskE"] = np.ascontiguousarray(mE)
    d["mask2"] = np.ascontiguousarray(m2)
    return d


def kernel(L0_attn_x, L0_attn_w_qkv, L0_attn_b_qkv, L0_attn_w_out, L0_attn_b_out,
           L0_attn_w_gate, L0_attn_b_gate, L0_attn_pos_bias, L0_ffn_w1,
           L0_ffn_conv_w, L0_ffn_conv_b, L0_ffn_w2):
    xF = np.ascontiguousarray(np.asarray(L0_attn_x, np.float32)[0].T)
    consts = make_consts(np.asarray(L0_attn_w_qkv, np.float32),
                         np.asarray(L0_attn_b_qkv, np.float32),
                         np.asarray(L0_attn_w_out, np.float32),
                         np.asarray(L0_attn_b_out, np.float32),
                         np.asarray(L0_attn_w_gate, np.float32),
                         np.asarray(L0_attn_b_gate, np.float32),
                         np.asarray(L0_attn_pos_bias, np.float32),
                         np.asarray(L0_ffn_w1, np.float32),
                         np.asarray(L0_ffn_conv_w, np.float32),
                         np.asarray(L0_ffn_conv_b, np.float32),
                         np.asarray(L0_ffn_w2, np.float32))
    nc = build_program()
    in_maps = [stage_core(c, xF, consts) for c in range(8)]
    res = run_bass_kernel_spmd(nc, in_maps, core_ids=list(range(8)))
    out = np.concatenate([np.asarray(r["out_xT"]).T for r in res.results], axis=0)
    return np.ascontiguousarray(out[None]).astype(np.float32)
